# version 42
# baseline (speedup 1.0000x reference)
import numpy as np
import ml_dtypes
from contextlib import ExitStack

import concourse.bass as bass
import concourse.mybir as mybir
from concourse.bass_utils import run_bass_kernel_spmd

F32 = mybir.dt.float32
BF16 = mybir.dt.bfloat16
U16 = mybir.dt.uint16
I32 = mybir.dt.int32
U8 = mybir.dt.uint8
AF = mybir.ActivationFunctionType
ALU = mybir.AluOpType

S = 4096
D = 1024
CH = 8
HP = 8
TB1 = 256
NT1 = S // TB1
SB = 512
NSB = S // SB
TB3 = 512
NT3 = S // TB3
EPS = 1e-5
KB = 1024
AW = 180 * KB

_bf = ml_dtypes.bfloat16


class Sem:
    __slots__ = ("name", "h", "n")

    def __init__(self, name):
        self.name = name
        self.h = None
        self.n = 0


class Prog:
    def __init__(self, engines, sems, active):
        self.engines = engines
        self.sems = sems
        self.active = active

    def _go(self, eng, method, args, kwargs, waits, sem, k):
        s = self.sems[sem]
        real = self.engines[eng] if eng == self.active else None
        for w in waits:
            if w is None:
                continue
            ws, wv = w
            wsem = self.sems[ws]
            assert 0 <= wv <= wsem.n, (ws, wv, wsem.n, method)
            if real is not None and wv > 0:
                real.wait_ge(wsem.h, wv)
        if real is not None:
            getattr(real, method)(*args, **kwargs).then_inc(s.h, k)
        s.n += k
        return s.n

    def t(self, method, *a, waits=(), **kw):
        return self._go("tensor", method, a, kw, waits, "pe", 1)

    def v(self, method, *a, waits=(), **kw):
        return self._go("vector", method, a, kw, waits, "dve", 1)

    def s(self, method, *a, waits=(), **kw):
        return self._go("scalar", method, a, kw, waits, "act", 1)

    def g(self, method, *a, waits=(), **kw):
        return self._go("gpsimd", method, a, kw, waits, "pool", 1)

    def dma(self, eng, out, in_, sem, waits=(), **kw):
        return self._go(eng, "dma_start", (out, in_), kw, waits, sem, 16)

    def wait(self, eng, *waits):
        real = self.engines[eng] if eng == self.active else None
        for ws, wv in waits:
            wsem = self.sems[ws]
            assert wv <= wsem.n, (ws, wv, wsem.n, "standalone")
            if real is not None:
                real.wait_ge(wsem.h, wv)


def schedule(P, T, dbg):
    pe, act, dve, pool = "pe", "act", "dve", "pool"

    cst = 0
    for dst, src in T["const_loads"]:
        cst = P.dma("sync", dst, src, "s_cst")
    cstw = 0
    for dst, src in T["castw_loads_p0"]:
        cstw = P.dma("gpsimd", dst, src, "s_cstw")
    n_cst = cst
    CST = ("s_cst", n_cst)
    CSTW = ("s_cstw", cstw)

    xT = T["xT"]
    wq, wkv = T["wq"], T["wkv"]
    Q, KV = T["Q"], T["KV"]
    LNx1 = T["LNx1"]
    onec, oner = T["ones_col"], T["ones_row"]
    onem = T["ones_mean"]
    psA, psB, psSa, psSb = T["psA"], T["psB"], T["psSa"], T["psSb"]
    psSc = T["psSc"]
    psc_bf = T["psc_bf"]
    rows_bf = T["rows_bf"]
    rows_f = T["rows_f"]

    lnst = {"xb": [0, 0], "xq": [0, 0], "ract": 0}
    rows_bufs = [rows_bf, T["rows_bf2"]]

    def emit_ln_stats(xsrc, N, xwaits, ps_s1, ps_s2, ps_rs, ps_nm,
                      cast_pool=False, rbuf=None):
        xbt, xsqt = T["xbt"], T["xsqt"]
        mm1 = mm2 = 0
        for c in range(CH):
            b = c & 1
            cw = list(xwaits)
            if lnst["xb"][b] > 0:
                cw.append((pe, lnst["xb"][b]))
            if cast_pool:
                mcast = P.g("tensor_copy", xbt[b][:, 0:N], xsrc[:, c, :],
                            waits=cw)
            else:
                mcast = P.v("tensor_copy", xbt[b][:, 0:N], xsrc[:, c, :],
                            waits=cw)
            sw = list(xwaits)
            if lnst["xq"][b] > 0:
                sw.append((pe, lnst["xq"][b]))
            msq = P.s("activation", xsqt[b][:, 0:N], xsrc[:, c, :],
                      AF.Square, waits=sw)
            w1 = [(pool if cast_pool else dve, mcast)]
            if c == 0:
                w1.append(CST)
            mm1 = P.t("matmul", ps_s1[0:1, 0:N], onem, xbt[b][:, 0:N],
                      start=(c == 0), stop=(c == CH - 1), waits=w1)
            lnst["xb"][b] = mm1
            mm2 = P.t("matmul", ps_s2[0:1, 0:N], onem, xsqt[b][:, 0:N],
                      start=(c == 0), stop=(c == CH - 1),
                      waits=[(act, msq)])
            lnst["xq"][b] = mm2
        if rbuf is None:
            rbuf = rows_bf
        m_f = rows_f[:, 0:N]
        q_f = rows_f[:, 512:512 + N]
        v_f = rows_f[:, 512 + N:512 + 2 * N]
        wfirst = [(pe, mm1)]
        if lnst["ract"]:
            wfirst.append((act, lnst["ract"]))
        mk = P.v("tensor_copy", m_f, ps_s1[0:1, 0:N], waits=wfirst)
        mk = P.v("tensor_tensor", v_f, m_f, m_f, ALU.mult, waits=[(dve, mk)])
        mk = P.v("tensor_tensor", v_f, ps_s2[0:1, 0:N], v_f, ALU.subtract,
                 waits=[(dve, mk), (pe, mm2)])
        ma = P.s("activation", q_f, v_f, AF.Ln, bias=T["cvals"][0:1, 0:1],
                 waits=[(dve, mk)])
        ma = P.s("activation", v_f, q_f, AF.Exp, scale=-0.5,
                 waits=[(act, ma)])
        lnst["ract"] = ma
        mk = P.v("scalar_tensor_tensor", q_f, m_f, -1.0, v_f,
                 ALU.mult, ALU.mult, waits=[(act, ma), (dve, mk)])
        mk = P.v("tensor_copy", rbuf[:, 0:2 * N],
                 rows_f[:, 512:512 + 2 * N], waits=[(dve, mk)])
        return mk

    def emit_ln_bcast(N, rows_mk, ps_rs, ps_nm, rbuf=None, extra=()):
        if rbuf is None:
            rbuf = rows_bf
        nm_b = rbuf[:, 0:N]
        rs_b = rbuf[:, N:2 * N]
        P.t("matmul", ps_rs[:, 0:N], oner, rs_b, start=True, stop=True,
            waits=[(dve, rows_mk), CST] + list(extra))
        mmn = P.t("matmul", ps_nm[:, 0:N], oner, nm_b, start=True,
                  stop=True)
        return mmn

    xf1 = T["xf1"]
    ld_sems = ["s_x1", "s_x2"]
    ld_mark = [0, 0]
    xf_free = [0, 0]
    evq_m = [0, 0]
    evk_m = [0, 0]
    vtm_ev = [0, 0]
    vtm_ev_d = [0, 0]
    vtm_st = [0, 0]
    lnx1_free = 0

    rows_m = [0] * NT1
    apply_m = [0, 0]

    def p1_load(t):
        b = t & 1
        w = []
        if apply_m[b] > 0:
            w.append((dve, apply_m[b]))
        ld_mark[b] = P.dma("sync", xf1[b],
                           xT.rearrange("(a p) t -> p a t", p=128)
                           [:, :, t * TB1:(t + 1) * TB1],
                           ld_sems[b], waits=w)

    def p1_stats(t):
        b = t & 1
        rows_m[t] = emit_ln_stats(xf1[b], TB1,
                                  [(ld_sems[b], ld_mark[b])],
                                  psSa[0:1, 256:512],
                                  psSb[0:1, 256:512], None, None,
                                  rbuf=rows_bufs[t & 1])

    for t in range(NT1):
        if t == 0:
            p1_load(0)
            p1_load(1)
            p1_stats(0)
            p1_stats(1)
        b = t & 1
        bcm = emit_ln_bcast(TB1, rows_m[t], psSa[:, 512:768],
                            psSb[:, 512:768], rbuf=rows_bufs[t & 1],
                            extra=([(dve, apply_m[b ^ 1])]
                                   if apply_m[b ^ 1] else []))
        rsB = psSa[:, 512:768].unsqueeze(1).broadcast_to([128, CH, TB1])
        nmB = psSb[:, 512:768].unsqueeze(1).broadcast_to([128, CH, TB1])
        mk = P.v("tensor_tensor", xf1[b], xf1[b], rsB, ALU.mult,
                 waits=[(pe, bcm), (ld_sems[b], ld_mark[b])])
        lw = [(dve, mk)]
        if lnx1_free > 0:
            lw.append((pe, lnx1_free))
        mk = P.v("tensor_tensor", LNx1, xf1[b], nmB, ALU.add, waits=lw)
        apply_m[b] = mk
        if t + 2 < NT1:
            p1_load(t + 2)
            p1_stats(t + 2)
        for dj in range(CH):
            pq = dj & 1
            psq = (psA, psB)[pq]
            w = [(dve, mk), CST, CSTW]
            if evq_m[pq] > 0:
                w.append((act, evq_m[pq]))
            mmq = 0
            for c in range(CH):
                mmq = P.t("matmul", psq[:, 0:TB1],
                          wq[:, c, dj * 128:(dj + 1) * 128],
                          LNx1[:, c, :], start=(c == 0), stop=(c == CH - 1),
                          waits=(w if c == 0 else ()))
            evq_m[pq] = P.s("activation", Q[:, dj, t * TB1:(t + 1) * TB1],
                            psq[:, 0:TB1], AF.Copy, scale=0.125,
                            waits=[(pe, mmq)])
            psk = (psSa, psSb)[pq]
            w2 = [(dve, mk)]
            if evk_m[pq] > 0:
                w2.append((dve, evk_m[pq]))
            mmk = 0
            for c in range(CH):
                mmk = P.t("matmul", psk[:, 0:TB1],
                          wkv[:, c, dj * 128:(dj + 1) * 128],
                          LNx1[:, c, :], start=(c == 0), stop=(c == CH - 1),
                          waits=(w2 if c == 0 else ()))
            evk_m[pq] = P.v("tensor_copy", KV[:, dj, t * TB1:(t + 1) * TB1],
                            psk[:, 0:TB1], waits=[(pe, mmk)])
            lnx1_free = mmk
        psc_v = T["psc_v"]
        for tch in range(TB1 // 128):
            tok0 = (t * 2 + tch) * 128
            mtv = 0
            for hp in range(HP):
                w = []
                if hp == 0:
                    w = [(dve, evk_m[0]), (dve, evk_m[1])]
                    for pq2 in range(2):
                        if vtm_ev_d[pq2]:
                            w.append((dve, vtm_ev_d[pq2]))
                mtv = P.t("transpose", psc_v[:, hp * 128:(hp + 1) * 128],
                          KV[:, hp, tok0:tok0 + 128], T["idb"], waits=w)
            for half in range(2):
                ev = P.v("tensor_copy", T["vtm_sb"][0][:, :],
                         psc_v[:, half * 512:(half + 1) * 512],
                         waits=[(pe, mtv)] + (
                             [("s_vt0", vtm_st[0])] if vtm_st[0] else []))
                vtm_ev_d[0] = ev
                vtm_st[0] = P.dma(
                    "sync", T["d_vtm"][tok0:tok0 + 128,
                                       half * 512:(half + 1) * 512],
                    T["vtm_sb"][0][:, :], "s_vt0",
                    waits=[(dve, ev)])

    p1_pe = P.sems["pe"].n
    p1_act = P.sems["act"].n
    p1_dve = P.sems["dve"].n

    if dbg == 1:
        dq = P.dma("sync", T["dbg_q"], Q[:, :, :], "s_y0",
                   waits=[(act, p1_act)])
        dk = P.dma("sync", T["dbg_kv"], KV[:, :, :], "s_y1",
                   waits=[(dve, p1_dve)])
        P.wait("sync", ("s_y0", dq), ("s_y1", dk))
        return

    mw = 0
    for dst, src in T["castw_loads_mid"]:
        mw = P.dma("gpsimd", dst, src, "s_w2",
                   waits=[(act, p1_act), (dve, p1_dve), (pe, p1_pe)])
    n_w2 = mw

    V_tm = T["V_tm"]
    sr = T["sr"]
    ao_sb = T["ao_sb"]
    d_ao = T["d_ao"]
    idb = T["idb"]

    av_prev_hp = 0
    sr_free = [0] * 3
    psS_free = [None] * 3
    ao_free = [0, 0]
    sao_sems = ["s_sao0", "s_sao1"]
    sao_cnt = [0, 0]
    av_ev_a = av_ev_b = 0
    vl_sems = ["s_vl0", "s_vl1"]
    vl_cnt = [0, 0]
    vt_rend = [("s_vt0", P.sems["s_vt0"].n), ("s_vt1", P.sems["s_vt1"].n)]

    def vtm_view(h):
        return T["d_vtm"].rearrange("(tc p) d -> p tc d", p=128)[
            :, :, h * 128:(h + 1) * 128]

    for hp in range(HP):
        if hp == 0:
            vl_cnt[0] = P.dma("sync", V_tm[0][:, :, :], vtm_view(0),
                              "s_vl0", waits=vt_rend + [(pe, p1_pe)])
            vl_cnt[1] = P.dma("sync", V_tm[1][:, :, :], vtm_view(1),
                              "s_vl1", waits=vt_rend + [(pe, p1_pe)])
        elif hp < HP - 1:
            hb = (hp + 1) & 1
            vl_cnt[hb] = P.dma("sync", V_tm[hb][:, :, :], vtm_view(hp + 1),
                               vl_sems[hb], waits=[(pe, av_prev_hp)])
        vtm = V_tm[hp & 1]
        vl_mark = (vl_sems[hp & 1], vl_cnt[hp & 1])

        for sb in range(NSB):
            sslc = slice(sb * SB, (sb + 1) * SB)
            av_mm = 0
            ev_marks = [None] * 32

            def emit_qkt(tc):
                pp = tc % 3
                psS = (psSa, psSb, psSc)[pp]
                w = []
                if psS_free[pp] is not None:
                    w.extend(psS_free[pp])
                if tc < 3:
                    if hp == 0 and sb == 0:
                        w.append((act, p1_act))
                        w.append((dve, p1_dve))
                        w.append((pe, p1_pe))
                P.t("matmul", psS[:, 0:512],
                    KV[0:64, hp, tc * 128:(tc + 1) * 128],
                    Q[0:64, hp, sslc], start=True, stop=True,
                    tile_position=(0, 0), waits=w)
                mmb = P.t("matmul", psS[:, 512:1024],
                          KV[64:128, hp, tc * 128:(tc + 1) * 128],
                          Q[64:128, hp, sslc], start=True, stop=True,
                          tile_position=(64, 0))
                ew = [(pe, mmb)]
                if sr_free[pp] > 0:
                    ew.append((pe, sr_free[pp]))
                mva = P.v("tensor_scalar", sr[pp][:, 0:512], psS[:, 0:512],
                          0.0, None, ALU.max, waits=ew)
                mvb = P.s("activation", sr[pp][:, 512:1024],
                          psS[:, 512:1024], AF.Relu, waits=ew)
                ev_marks[tc] = (mva, mvb)
                psS_free[pp] = [(dve, mva), (act, mvb)]

            def emit_av(tc):
                nonlocal av_mm
                pp = tc % 3
                mva, mvb = ev_marks[tc]
                w2 = [(dve, mva), (act, mvb)]
                if tc == 0:
                    w2.append(vl_mark)
                    if av_ev_a:
                        w2.append((dve, av_ev_a))
                    if av_ev_b:
                        w2.append((act, av_ev_b))
                P.t("matmul", psA[0:64, :], vtm[:, tc, 0:64],
                    sr[pp][:, 0:512], start=(tc == 0), stop=(tc == 31),
                    tile_position=(0, 0), waits=w2)
                av_mm = P.t("matmul", psB[64:128, :], vtm[:, tc, 64:128],
                            sr[pp][:, 512:1024], start=(tc == 0),
                            stop=(tc == 31), tile_position=(0, 64))
                sr_free[pp] = av_mm

            emit_qkt(0)
            emit_qkt(1)
            for tc in range(2, 32):
                emit_qkt(tc)
                emit_av(tc - 2)
            emit_av(30)
            emit_av(31)
            pq = sb & 1
            ew = [(pe, av_mm)]
            if ao_free[pq] > 0:
                ew.append((sao_sems[pq], ao_free[pq]))
            av_ev_a = P.v("tensor_copy", ao_sb[pq][0:64, :], psA[0:64, :],
                          waits=ew)
            av_ev_b = P.s("copy", ao_sb[pq][64:128, :],
                          psB[64:128, :], waits=ew)
            sao_cnt[pq] = P.dma("sync",
                                d_ao[hp * 128:(hp + 1) * 128, sslc],
                                ao_sb[pq][:, :], sao_sems[pq],
                                waits=[(dve, av_ev_a), (act, av_ev_b)])
            ao_free[pq] = sao_cnt[pq]
        av_prev_hp = av_mm

    attn_pe = P.sems["pe"].n
    attn_dve = P.sems["dve"].n
    attn_act = P.sems["act"].n

    woutb = T["woutb"]
    colsum_b = T["colsum_b"]
    nwb = P.dma("gpsimd", woutb, T["woutb_d"].rearrange(
        "(a p) d -> p a d", p=128), "s_wb", waits=[(pe, attn_pe)])
    mcs = 0
    cm = 0
    for half in range(2):
        dst = (psSa, psSb)[half]
        for c in range(CH):
            w = ()
            if c == 0:
                w = [("s_wb", nwb), (dve, attn_dve), (act, attn_act)]
            cm = P.t("matmul", dst[0:1, 0:512], onec,
                     woutb[:, c, half * 512:(half + 1) * 512],
                     start=(c == 0), stop=(c == CH - 1), waits=w)
        mcs = P.v("tensor_copy", colsum_b[:, half * 512:(half + 1) * 512],
                  dst[0:1, 0:512],
                  waits=[(pe, cm)] + ([(dve, mcs)] if mcs else []))

    w3 = {}
    for (name, dst, src) in T["castw_loads_late"]:
        w3[name] = P.dma("gpsimd", dst, src, "s_" + name,
                         waits=[(pe, attn_pe), (pe, cm)])

    if dbg == 2:
        dq = P.dma("sync", T["dbg_cs"], colsum_b[:, :], "s_y0",
                   waits=[(dve, mcs)])
        P.wait("sync", ("s_y0", dq),
               ("s_sao0", sao_cnt[0]), ("s_sao1", sao_cnt[1]))
        return

    aot = T["aot"]
    xf3 = T["xf3"]
    d_x1 = T["d_x1"]
    wout = T["wout"]
    v_bout = T["v_bout"]
    evp = [("dve", attn_dve), ("act", attn_act)]
    evpC = [psS_free[2][0], psS_free[2][1]]
    banksA = (T["psA"][:, 0:TB3], T["psB"][:, 0:TB3])
    banksC = (T["psSc"][:, 0:512], T["psSc"][:, 512:1024])
    aotb = [aot, T["aot2"]]
    ao_sems = ["s_ao3a", "s_ao3b"]
    ao_lds = [0, 0]
    aot_freem = [0, 0]
    x1_st = [0, 0]
    x1_ev = [0, 0]
    ld25 = [0, 0]
    n_r2 = 0

    def ao_load(sb):
        ab = sb & 1
        aw = [("s_sao0", sao_cnt[0]), ("s_sao1", sao_cnt[1]),
              (pe, attn_pe)]
        if aot_freem[ab] > 0:
            aw.append((pe, aot_freem[ab]))
        ao_lds[ab] = P.dma("sync", aotb[ab],
                           d_ao.rearrange("(a p) t -> p a t", p=128)
                           [:, :, sb * SB:(sb + 1) * SB],
                           ao_sems[ab], waits=aw)

    for sb in range(NSB):
        b = sb & 1
        sslc = slice(sb * SB, (sb + 1) * SB)
        if sb == 0:
            ao_load(0)
            ao_load(1)
        elif sb + 1 < NSB:
            ao_load(sb + 1)
        w = []
        if x1_st[b] > 0:
            w.append((("s_y0", "s_y1")[b], x1_st[b]))
        ld25[b] = P.dma("sync", xf3[b],
                        xT.rearrange("(a p) t -> p a t", p=128)[:, :, sslc],
                        ("s_x3a", "s_x3b")[b], waits=w)
        last = 0
        for dj in range(CH):
            pq = dj & 1
            psq = (psA, psB)[pq]
            w = [(ao_sems[b], ao_lds[b]), ("s_w2", n_w2),
                 (("s_x3a", "s_x3b")[b], ld25[b]), evp[pq]]
            mm = 0
            for c in range(CH):
                mm = P.t("matmul", psq[:, 0:SB],
                         wout[:, c, dj * 128:(dj + 1) * 128],
                         aotb[b][:, c, :], start=(c == 0),
                         stop=(c == CH - 1),
                         waits=(w if c == 0 else ()))
            if dj == CH - 1:
                aot_freem[b] = mm
            last = P.v("scalar_tensor_tensor", xf3[b][:, dj, :],
                       psq[:, 0:SB], v_bout[:, dj:dj + 1], xf3[b][:, dj, :],
                       ALU.add, ALU.add, waits=[(pe, mm)])
            evp[pq] = (dve, last)
        x1_ev[b] = last
        x1_st[b] = P.dma("sync", d_x1.rearrange("(a p) t -> p a t", p=128)
                         [:, :, sslc], xf3[b], ("s_y0", "s_y1")[b],
                         waits=[(dve, last)])
        rw = [(dve, last)]
        if sb > 0:
            rw.append(("s_x1s", n_r2))
        rmk = emit_ln_stats(xf3[b], SB, rw,
                            psSa[0:1, 512:1024], psSb[0:1, 512:1024],
                            None, None, rbuf=rows_bufs[sb & 1])
        n_r2 = P.dma("sync", T["d_r2"][0:1, sb * 1024:(sb + 1) * 1024],
                     rows_bufs[sb & 1][:, 0:1024], "s_x1s",
                     waits=[(dve, rmk)])
        rows_free = rmk
    x1_rend = [("s_y0", x1_st[0]), ("s_y1", x1_st[1])]
    r2_rend = ("s_x1s", n_r2)

    LNxp = T["LNx"]
    wrT, wkT, wvT = T["wrT"], T["wkT"], T["wvT"]
    wcat = T["wcat"]
    bcat = T["bcat"]
    v_boutb = T["v_boutb"]
    v_mur, v_muk = T["v_mur"], T["v_muk"]
    bcol2, bcol4 = T["bcol2"], T["bcol4"]
    idf = T["idf"]
    tab = T["tab"]
    gout = T["gout"]
    idxu = T["idxu"]
    extmask = T["extmask"]
    blkd8, blkdT8 = T["blkd8"], T["blkdT8"]
    x0row = T["x0row"]
    d_x0 = T["d_x0"]
    gtm_bf = T["gtm_bf"]
    gfm = T["gfm"]
    e_bf, R_bf, gw_bf, s_bf = T["e_bf"], T["R_bf"], T["gw_bf"], T["s_bf"]
    ta, tb_, tc2, td, tneg = T["ta"], T["tb"], T["tc2"], T["td"], T["tneg"]
    cint = T["cint"]
    yT = T["yT"]

    mz = 0
    for bc_t in (bcol2[0], bcol2[1], bcol4[0], bcol4[1]):
        mz = P.v("memset", bc_t, 0.0, waits=[(dve, attn_dve)])
    mz_done = mz

    ld3_sems = ["s_x3a", "s_x3b"]
    ld3_mark = list(ld25)
    xf3_free = list(x1_st)
    y_sems = ["s_y0", "s_y1"]
    y_cnt = list(x1_st)
    sa1 = ("dve", mcs)
    sb1 = ("dve", mcs)
    sc1 = ("act", attn_act)
    sc2 = ("act", attn_act)
    tab_ready = 0
    st8 = {}

    r2_sems = ["s_r2a", "s_r2b"]
    r2_mark = [0, 0]
    r2_free = [0, 0]

    def S_loads(t):
        b = t & 1
        w = list(x1_rend)
        if xf3_free[b] > 0:
            w.append((y_sems[b], xf3_free[b]))
        ld3_mark[b] = P.dma("sync", xf3[b],
                            d_x1.rearrange("(a p) t -> p a t", p=128)
                            [:, :, t * TB3:(t + 1) * TB3],
                            ld3_sems[b], waits=w)
        rw = [r2_rend]
        if r2_free[b] > 0:
            rw.append((pe, r2_free[b]))
        r2_mark[b] = P.dma("sync", T["rows2"][b],
                           T["d_r2"][0:1, t * 1024:(t + 1) * 1024],
                           r2_sems[b], waits=rw)

    def emit_ln(t, xwaits):
        x = xf3[t & 1]
        lnx = LNxp[t & 1]
        rmk = emit_ln_stats(x, TB3, xwaits,
                            psSa[0:1, 512:1024], psSb[0:1, 512:1024],
                            None, None, cast_pool=False)
        bc = emit_ln_bcast(TB3, rmk, psSa[:, 512:1024],
                           psSb[:, 512:1024])
        rsB = psSa[:, 512:1024].unsqueeze(1).broadcast_to([128, CH, TB3])
        nmB = psSb[:, 512:1024].unsqueeze(1).broadcast_to([128, CH, TB3])
        mk = P.v("tensor_tensor", lnx, x, rsB, ALU.mult,
                 waits=[(pe, bc), (dve, mz_done)] + xwaits)
        mk = P.v("tensor_tensor", lnx, lnx, nmB, ALU.add,
                 waits=[(dve, mk)])
        return mk

    def emit_ffn(t, ln_m, bc_rd, bc_wr, xin_m, banks, evp):
        b = t & 1
        x = xf3[b]
        lnx = LNxp[b]
        xr, xk = T["xr"][b], T["xk"][b]
        kr, r_bf16 = T["kr"][b], T["r_bf16"][b]
        mt = P.v("tensor_copy", bc_wr, lnx[:, :, TB3 - 1:TB3],
                 waits=[(dve, ln_m)])
        md = P.v("tensor_tensor", kr[:, :, 1:TB3], lnx[:, :, 1:TB3],
                 lnx[:, :, 0:TB3 - 1], ALU.subtract, waits=[(dve, mt)])
        md = P.v("tensor_tensor", kr[:, :, 0:1], lnx[:, :, 0:1],
                 bc_rd, ALU.subtract, waits=[(dve, md)])
        msc = 0
        mxr = md
        for c in range(CH):
            ew = [(dve, md)]
            if msc:
                ew.append((act, msc))
            msc = P.s("activation", xr[:, c, 1:TB3], kr[:, c, 1:TB3],
                      AF.Copy, scale=v_mur[:, c:c + 1], waits=ew)
            mxr = P.v("tensor_tensor", xr[:, c, 1:TB3], xr[:, c, 1:TB3],
                      lnx[:, c, 0:TB3 - 1], ALU.add,
                      waits=[(act, msc), (dve, mxr)])
            mxr = P.v("scalar_tensor_tensor", xr[:, c, 0:1],
                      kr[:, c, 0:1], v_mur[:, c:c + 1],
                      bc_rd[:, c, :], ALU.mult, ALU.add,
                      waits=[(dve, mxr)])
        mxk = mxr
        for c in range(CH):
            msc = P.s("activation", xk[:, c, 1:TB3], kr[:, c, 1:TB3],
                      AF.Copy, scale=v_muk[:, c:c + 1],
                      waits=[(act, msc), (dve, md)])
            mxk = P.v("tensor_tensor", xk[:, c, 1:TB3], xk[:, c, 1:TB3],
                      lnx[:, c, 0:TB3 - 1], ALU.add,
                      waits=[(act, msc), (dve, mxk)])
            mxk = P.v("scalar_tensor_tensor", xk[:, c, 0:1],
                      kr[:, c, 0:1], v_muk[:, c:c + 1],
                      bc_rd[:, c, :], ALU.mult, ALU.add,
                      waits=[(dve, mxk)])
        mex = 0
        for dj in range(CH):
            pq = dj & 1
            w = [(dve, mxr), ("s_wrT", w3["wrT"]), evp[pq]]
            mm = 0
            for c in range(CH):
                mm = P.t("matmul", banks[pq],
                         wrT[:, c, dj * 128:(dj + 1) * 128],
                         xr[:, c, :], start=(c == 0), stop=(c == CH - 1),
                         waits=(w if c == 0 else ()))
            ew = [(pe, mm)]
            if mex:
                ew.append((act, mex))
            mex = P.s("activation", r_bf16[:, dj, :],
                      banks[pq], AF.Exp, scale=-1.0, waits=ew)
            evp[pq] = (act, mex)
        mkr = 0
        for dj in range(CH):
            pq = dj & 1
            w = [(dve, mxk), ("s_wkT", w3["wkT"]), evp[pq]]
            mm = 0
            for c in range(CH):
                mm = P.t("matmul", banks[pq],
                         wkT[:, c, dj * 128:(dj + 1) * 128],
                         xk[:, c, :], start=(c == 0), stop=(c == CH - 1),
                         waits=(w if c == 0 else ()))
            mkr = P.s("activation", kr[:, dj, :],
                      banks[pq], AF.Relu,
                      waits=[(pe, mm), (dve, mxk) if dj == 0 else None])
            evp[pq] = (act, mkr)
        mln = P.s("activation", r_bf16, r_bf16, AF.Ln,
                  bias=T["cvals"][:, 1:2], waits=[(act, mkr)])
        mr = P.s("activation", r_bf16, r_bf16, AF.Exp, scale=-1.0,
                 waits=[(act, mln)])
        mka = P.v("tensor_tensor", kr, kr, kr, ALU.mult,
                  waits=[(act, mkr)])
        mfin = 0
        for dj in range(CH):
            pq = dj & 1
            w = [(dve, mka), ("s_wvT", w3["wvT"]), evp[pq]]
            mm = 0
            for c in range(CH):
                mm = P.t("matmul", banks[pq],
                         wvT[:, c, dj * 128:(dj + 1) * 128],
                         kr[:, c, :], start=(c == 0), stop=(c == CH - 1),
                         waits=(w if c == 0 else ()))
            mt1 = P.v("scalar_tensor_tensor", xr[:, dj, :],
                      r_bf16[:, dj, :], 1.0, banks[pq],
                      ALU.mult, ALU.mult,
                      waits=[(pe, mm), (act, mr), (dve, xin_m),
                             (dve, mka)])
            mfin = P.v("tensor_tensor", x[:, dj, :], x[:, dj, :],
                       xr[:, dj, :], ALU.add, waits=[(dve, mt1)])
            evp[pq] = (dve, mfin)
        return mfin

    def S5a(t):
        nonlocal sa1, sc1, tab_ready
        b = t & 1
        lnx = LNxp[b]
        ln3 = st8[(t, "ln3")]
        if t == 0:
            mrow = 0
            for a in range(CH):
                wv = [(dve, ln3), sc1] if a == 0 else [(dve, mrow)]
                mtp = P.t("transpose", psc_bf[0:1, 0:128],
                          lnx[:, a, 0:1], idb, waits=wv)
                mrow = P.v("tensor_copy", x0row[:, a * 128:(a + 1) * 128],
                           psc_bf[0:1, 0:128], waits=[(pe, mtp)])
            sc1 = ("dve", mrow)
            m1 = P.dma("sync", d_x0, x0row[:, :], "s_x0",
                       waits=[(dve, mrow)])
            tab_ready = P.dma("sync", tab[:, :],
                              d_x0[0:1, :].broadcast_to([128, 1024]),
                              "s_x0", waits=[("s_x0", m1)])
        w = [(dve, ln3), ("s_w2", n_w2), sa1]
        if t == 0:
            w.append(sc1)
        mm = 0
        for c in range(CH):
            mm = P.t("matmul", psSa[0:64, 0:512], wcat[:, c, :],
                     lnx[:, c, :], start=(c == 0), stop=(c == CH - 1),
                     waits=(w if c == 0 else ()))
        moff = P.v("tensor_scalar", ta[0:32, :], psSa[0:32, 0:512],
                   bcat[0:32, :], None, ALU.add, waits=[(pe, mm)])
        me = P.s("activation", tb_[0:32, :], psSa[32:64, 0:512], AF.Exp,
                 bias=bcat[32:64, :], waits=[(pe, mm)])
        st8[(t, "sa1r")] = [(dve, moff), (act, me)]
        mk = P.v("tensor_copy", cint[0:32, :], ta[0:32, :],
                 waits=[(dve, moff)])
        mk = P.v("tensor_copy", tc2[0:32, :], cint[0:32, :],
                 waits=[(dve, mk)])
        mk = P.v("tensor_tensor", td[0:32, :], tc2[0:32, :], ta[0:32, :],
                 ALU.is_gt, waits=[(dve, mk)])
        mk = P.v("tensor_tensor", tc2[0:32, :], tc2[0:32, :], td[0:32, :],
                 ALU.subtract, waits=[(dve, mk)])
        mk = P.v("tensor_tensor", td[0:32, :], tc2[0:32, :], ta[0:32, :],
                 ALU.not_equal, waits=[(dve, mk)])
        mk = P.v("tensor_scalar", tneg[0:32, :], ta[0:32, :], 0.0, None,
                 ALU.is_lt, waits=[(dve, mk)])
        mk = P.v("tensor_tensor", td[0:32, :], td[0:32, :], tneg[0:32, :],
                 ALU.mult, waits=[(dve, mk)])
        mk = P.v("tensor_tensor", ta[0:32, :], tc2[0:32, :], td[0:32, :],
                 ALU.add, waits=[(dve, mk)])
        mk = P.v("tensor_scalar", td[0:32, :], ta[0:32, :], 0.0, None,
                 ALU.is_lt, waits=[(dve, mk)])
        mk = P.v("scalar_tensor_tensor", ta[0:32, :], td[0:32, :], 1024.0,
                 ta[0:32, :], ALU.mult, ALU.add, waits=[(dve, mk)])
        midx = P.v("tensor_scalar", ta[0:32, :], ta[0:32, :], 1023.0, 0.0,
                   ALU.min, ALU.max, waits=[(dve, mk)])
        nonlocal sc2
        mtp_prev = 0
        tp_free = [sc1, sc2]
        for q in range(4):
            bsl = (psSa[:, 64:96], psSb[:, 64:96])[q & 1]
            wv = [tp_free[q & 1]]
            if q == 0:
                wv.append((dve, midx))
                wv.append(st8[(t, "sa1r")][1])
            mtp = P.t("transpose", bsl,
                      ta[0:32, q * 128:(q + 1) * 128], idf[:, :],
                      waits=wv)
            mtp_prev = P.v("tensor_copy", idxu[:, q, :], bsl,
                           waits=[(pe, mtp)] + ([(dve, mtp_prev)]
                                                if mtp_prev else []))
            tp_free[q & 1] = (dve, mtp_prev)
        sc1 = ("dve", mtp_prev)
        sc2 = tp_free[1]
        nonlocal sb1
        me = st8[(t, "sa1r")][1]
        meb = P.v("tensor_copy", e_bf[0:32, :], tb_[0:32, :], waits=[me])
        mS = P.t("matmul", psSb[0:8, 0:512], blkd8, e_bf[0:32, :],
                 start=True, stop=True, waits=[(dve, meb), sb1])
        mlnS = P.s("activation", td[0:8, :], psSb[0:8, 0:512], AF.Ln,
                   waits=[(pe, mS), (dve, midx)])
        mR = P.s("activation", tc2[0:8, :], td[0:8, :], AF.Exp, scale=-1.0,
                 waits=[(act, mlnS), (dve, mtp_prev)])
        mRb = P.v("tensor_copy", R_bf[0:8, :], tc2[0:8, :],
                  waits=[(act, mR)])
        mRB = P.t("matmul", psSb[0:32, 0:512], blkdT8, R_bf[0:8, :],
                  start=True, stop=True, waits=[(dve, mRb), (act, mlnS)])
        sb1 = ("dve", mRb)
        st8[(t, "meb")] = meb
        st8[(t, "mR")] = mR
        st8[(t, "mRB")] = mRB
        gq = T["gq"]
        mmul = 0
        for q in range(4):
            wv = [(dve, mtp_prev), ("s_x0", tab_ready)]
            if mmul:
                wv.append((pool, mmul))
            mgth = P.g("indirect_copy", gout[:, :], tab[:, :],
                       idxu[:, q, :], True, waits=wv)
            mmul = P.g("tensor_tensor", gq[:, q, :], gout[:, :],
                       extmask[:, :], ALU.mult,
                       waits=[(pool, mgth), ("s_em", w3["em"])])
        st8[(t, "gmul")] = mmul

    def S5b(t):
        nonlocal sa1, sb1, sc1, sc2, evp
        b = t & 1
        x = xf3[b]
        mred = 0
        for q in range(4):
            mred = P.v("tensor_reduce", T["gtm32"][:, q * 32:(q + 1) * 32],
                       T["gq"][:, q, :].rearrange("p (h q) -> p h q", q=16),
                       mybir.AxisListType.X, ALU.add,
                       waits=[(pool, st8[(t, "gmul")])] +
                             ([(dve, mred)] if mred else []))
        gcast = P.v("tensor_copy", gtm_bf[:, 0:128], T["gtm32"][:, 0:128],
                    waits=[(dve, mred)])
        ffn1_m = st8[(t, "ffn1")]
        mgf = 0
        for q in range(4):
            wv = [(dve, gcast), sc1]
            if mgf:
                wv.append((dve, mgf))
            mtp = P.t("transpose", psc_bf[0:32, 0:128],
                      gtm_bf[:, q * 32:(q + 1) * 32], idb, waits=wv)
            mgf = P.v("tensor_copy", gfm[:, q * 128:(q + 1) * 128],
                      psc_bf[0:32, 0:128], waits=[(pe, mtp)])
        sc1 = ("dve", mgf)
        meb = st8[(t, "meb")]
        mRB = st8[(t, "mRB")]
        mR = st8[(t, "mR")]
        mk = P.v("tensor_tensor", td[0:32, :], gfm[:, :], e_bf[0:32, :],
                 ALU.mult, waits=[(dve, mgf), (dve, meb), (act, mR)])
        mk = P.v("tensor_tensor", gw_bf[0:32, :], td[0:32, :],
                 psSb[0:32, 0:512], ALU.mult, waits=[(dve, mk), (pe, mRB)])
        ms = P.t("matmul", psSb[0:1, 0:512], onec[0:32, :],
                 gw_bf[0:32, :], start=True, stop=True, waits=[(dve, mk)])
        msb = P.v("tensor_copy", s_bf[:, :], psSb[0:1, 0:512],
                  waits=[(pe, ms)])
        sb1 = ("dve", msb)
        mab = 0
        for dj in range(CH):
            po = dj & 1
            dst = (psSa[:, 0:512], psSb[:, 0:512])[po]
            w = [(dve, msb), (dve, mcs), (sa1 if po == 0 else sc2), CST]
            if dj < 2:
                w.extend(st8[(t, "sa1r")])
            mm = P.t("matmul", dst, colsum_b[:, dj * 128:(dj + 1) * 128],
                     s_bf[:, :], start=True, stop=True, waits=w)
            mab = P.v("scalar_tensor_tensor", x[:, dj, :], dst,
                      v_boutb[:, dj:dj + 1], x[:, dj, :], ALU.add, ALU.add,
                      waits=[(pe, mm), (dve, ffn1_m)])
            if po == 0:
                sa1 = ("dve", mab)
            else:
                sc2 = ("dve", mab)
        st8[(t, "mab")] = mab

    def S8(t):
        b = t & 1
        y_cnt[b] = P.dma("sync",
                         yT.rearrange("(a p) t -> p a t", p=128)
                         [:, :, t * TB3:(t + 1) * TB3],
                         xf3[b], y_sems[b],
                         waits=[(dve, st8[(t, "ffn2")])])
        xf3_free[b] = y_cnt[b]

    def emit_ln2(t, prev_mk):
        b = t & 1
        x = xf3[b]
        lnx = LNxp[b]
        rw = [(r2_sems[b], r2_mark[b]), (dve, mz_done)]
        if prev_mk is not None:
            rw.append((dve, prev_mk))
        P.t("matmul", psSa[:, 512:1024], oner, T["rows2"][b][:, 512:1024],
            start=True, stop=True, waits=rw + [CST])
        bc = P.t("matmul", psSb[:, 512:1024], oner,
                 T["rows2"][b][:, 0:512], start=True, stop=True)
        r2_free[b] = bc
        rsB = psSa[:, 512:1024].unsqueeze(1).broadcast_to([128, CH, TB3])
        nmB = psSb[:, 512:1024].unsqueeze(1).broadcast_to([128, CH, TB3])
        mk = P.v("tensor_tensor", lnx, x, rsB, ALU.mult,
                 waits=[(pe, bc), (ld3_sems[b], ld3_mark[b]),
                        (dve, mz_done)])
        mk = P.v("tensor_tensor", lnx, lnx, nmB, ALU.add,
                 waits=[(dve, mk)])
        return mk

    S_loads(0)
    S_loads(1)
    st8[(0, "ln2")] = emit_ln2(0, rows_free)
    st8[(0, "ffn1")] = emit_ffn(0, st8[(0, "ln2")], bcol2[0], bcol2[1],
                                st8[(0, "ln2")], banksA, evp)
    st8[(0, "ln3")] = emit_ln(0, [(dve, st8[(0, "ffn1")])])
    S5a(0)
    for t in range(1, NT3):
        st8[(t, "ln2")] = emit_ln2(t, st8[(t - 1, "ln3")])
        st8[(t, "ffn1")] = emit_ffn(t, st8[(t, "ln2")], bcol2[t & 1],
                                    bcol2[(t + 1) & 1], st8[(t, "ln2")],
                                    banksA, evp)
        S5b(t - 1)
        st8[(t - 1, "ln4")] = emit_ln(t - 1, [(dve, st8[(t - 1, "mab")])])
        st8[(t - 1, "ffn2")] = emit_ffn(t - 1, st8[(t - 1, "ln4")],
                                        bcol4[(t - 1) & 1], bcol4[t & 1],
                                        st8[(t - 1, "mab")], banksC, evpC)
        S8(t - 1)
        if t + 1 < NT3:
            S_loads(t + 1)
        st8[(t, "ln3")] = emit_ln(t, [(dve, st8[(t, "ffn1")])])
        S5a(t)
    tl = NT3 - 1
    S5b(tl)
    st8[(tl, "ln4")] = emit_ln(tl, [(dve, st8[(tl, "mab")])])
    st8[(tl, "ffn2")] = emit_ffn(tl, st8[(tl, "ln4")],
                                 bcol4[tl & 1], bcol4[(tl + 1) & 1],
                                 st8[(tl, "mab")], banksC, evpC)
    S8(tl)

    P.wait("sync", ("s_y0", y_cnt[0]), ("s_y1", y_cnt[1]))


def build_nc(dbg=0):
    nc = bass.Bass("TRN2", target_bir_lowering=False)
    T = {}

    def din(name, shape, dt=F32):
        return nc.dram_tensor(name, shape, dt, kind="ExternalInput").ap()

    T["xT"] = din("xT", [D, S])
    for nm in ("wq_d", "wkv_d", "wout_d", "wrT_d", "wkT_d", "wvT_d",
               "woutb_d"):
        T[nm] = din(nm, [D, D])
    T["wcat_d"] = din("wcat_d", [D, 64])
    T["bcat_d"] = din("bcat_d", [64, 1])
    T["bout_d"] = din("bout_d", [128, CH])
    T["boutb_d"] = din("boutb_d", [128, CH])
    T["mur_d"] = din("mur_d", [128, CH])
    T["muk_d"] = din("muk_d", [128, CH])
    T["idb_d"] = din("idb_d", [128, 128], BF16)
    T["idf_d"] = din("idf_d", [32, 32], F32)
    T["onec_d"] = din("onec_d", [128, 1], BF16)
    T["onem_d"] = din("onem_d", [128, 1], BF16)
    T["oner_d"] = din("oner_d", [1, 128], BF16)
    T["blkd8_d"] = din("blkd8_d", [32, 8], BF16)
    T["blkdT8_d"] = din("blkdT8_d", [8, 32], BF16)
    T["extmask_d"] = din("extmask_d", [128, 512], BF16)
    T["cvals_d"] = din("cvals_d", [128, 2], F32)

    T["yT"] = nc.dram_tensor("yT", [D, S], F32, kind="ExternalOutput").ap()
    T["d_ao"] = nc.dram_tensor("d_ao", [D, S], BF16,
                               kind=("ExternalOutput" if dbg == 2 else "Internal")).ap()
    T["d_x0"] = nc.dram_tensor("d_x0", [1, D], BF16, kind="Internal").ap()
    T["d_vtm"] = nc.dram_tensor("d_vtm", [S, D], BF16, kind="Internal").ap()
    T["d_x1"] = nc.dram_tensor("d_x1", [D, S], F32, kind="Internal").ap()
    T["d_r2"] = nc.dram_tensor("d_r2", [1, 2 * S], BF16, kind="Internal").ap()
    if dbg == 1:
        T["dbg_q"] = nc.dram_tensor("dbg_q", [128, CH, S], BF16,
                                    kind="ExternalOutput").ap()
        T["dbg_kv"] = nc.dram_tensor("dbg_kv", [128, CH, S], BF16,
                                     kind="ExternalOutput").ap()
    if dbg == 2:
        T["dbg_cs"] = nc.dram_tensor("dbg_cs", [1, D], BF16,
                                     kind="ExternalOutput").ap()

    es = ExitStack()
    with es:
        arena = es.enter_context(nc.sbuf_tensor("arena", [128, AW], U8))
        ar = arena[:]

        def carve(off_kb, shape, dt):
            isz = {F32: 4, BF16: 2, U16: 2, I32: 4}[dt]
            n = 1
            for d in shape[1:]:
                n *= d
            off = int(off_kb * KB)
            assert off + n * isz <= AW, (off_kb, shape)
            v = ar[:, off:off + n * isz].bitcast(dt)
            if len(shape) == 3:
                v = v.rearrange("p (a n) -> p a n", a=shape[1])
            return v

        T["Q"] = carve(0, [128, CH, S], BF16)
        T["xf3"] = [carve(0, [128, CH, TB3], F32),
                    carve(16, [128, CH, TB3], F32)]
        T["LNx"] = [carve(32, [128, CH, TB3], BF16),
                    carve(40, [128, CH, TB3], BF16)]
        T["xr"] = [carve(48, [128, CH, TB3], BF16),
                   carve(56, [128, CH, TB3], BF16)]
        T["KV"] = carve(64, [128, CH, S], BF16)
        T["wrT"] = carve(64, [128, CH, D], BF16)
        T["wkT"] = carve(80, [128, CH, D], BF16)
        T["wvT"] = carve(96, [128, CH, D], BF16)
        T["xk"] = [carve(112, [128, CH, TB3], BF16),
                    carve(120, [128, CH, TB3], BF16)]
        T["aot"] = carve(112, [128, CH, TB3], BF16)
        T["aot2"] = carve(120, [128, CH, TB3], BF16)
        T["wq"] = carve(128, [128, CH, D], BF16)
        T["wkv"] = carve(144, [128, CH, D], BF16)
        T["xf1"] = [carve(160, [128, CH, TB1], F32),
                    carve(168, [128, CH, TB1], F32)]
        T["LNx1"] = carve(176, [128, CH, TB1], BF16)
        T["V_tm"] = [carve(128, [128, 32, 128], BF16),
                     carve(136, [128, 32, 128], BF16)]
        T["sr"] = [carve(144, [128, 1024], BF16),
                   carve(146, [128, 1024], BF16),
                   carve(148, [128, 1024], BF16)]
        T["ao_sb"] = [carve(150, [128, 512], BF16),
                      carve(151, [128, 512], BF16)]
        T["wout"] = carve(152, [128, CH, D], BF16)
        T["wcat"] = carve(172, [128, CH, 64], BF16)
        T["woutb"] = carve(64, [128, CH, D], BF16)
        T["kr"] = [carve(128, [128, CH, TB3], BF16),
                   carve(136, [128, CH, TB3], BF16)]
        T["r_bf16"] = [carve(144, [128, CH, TB3], BF16),
                       carve(152, [128, CH, TB3], BF16)]
        T["ta"] = carve(160, [128, TB3], F32)
        T["tb"] = carve(162, [128, TB3], F32)
        T["tc2"] = carve(164, [128, TB3], F32)
        T["td"] = carve(166, [128, TB3], F32)
        T["cint"] = carve(166, [128, TB3], I32)
        T["tneg"] = carve(168, [128, TB3], BF16)
        T["s_bf"] = carve(169, [128, 512], BF16)[0:1, :]
        T["colsum_b"] = carve(178, [128, 1024], BF16)[0:1, :]
        T["gfm"] = carve(173, [128, 512], BF16)[0:32, :]
        T["gw_bf"] = T["gfm"]
        T["x0row"] = carve(170, [128, 1024], BF16)[0:1, :]
        T["rows2"] = [carve(174, [128, 1024], BF16)[0:1, :],
                      carve(176, [128, 1024], BF16)[0:1, :]]

        def sb(name, shape, dt):
            return es.enter_context(nc.sbuf_tensor(name, shape, dt))[:]

        T["idb"] = sb("idb", [128, 128], BF16)
        T["idf"] = sb("idf", [32, 32], F32)
        T["ones_col"] = sb("ones_col", [128, 1], BF16)
        T["ones_mean"] = sb("ones_mean", [128, 1], BF16)
        T["ones_row"] = sb("ones_row", [1, 128], BF16)
        T["blkd8"] = sb("blkd8", [32, 8], BF16)
        T["blkdT8"] = sb("blkdT8", [8, 32], BF16)
        T["cvals"] = sb("cvals", [128, 2], F32)
        T["bcat"] = sb("bcat", [64, 1], F32)
        T["v_bout"] = sb("v_bout", [128, CH], F32)
        T["v_boutb"] = sb("v_boutb", [128, CH], F32)
        T["v_mur"] = sb("v_mur", [128, CH], F32)
        T["v_muk"] = sb("v_muk", [128, CH], F32)
        T["rows_bf"] = sb("rows_bf", [1, 1024], BF16)
        T["rows_bf2"] = sb("rows_bf2", [1, 1024], BF16)
        T["rows_f"] = sb("rows_f", [1, 1536], F32)
        T["xbt"] = [sb("xbt0", [128, 512], BF16),
                    sb("xbt1", [128, 512], BF16)]
        T["xsqt"] = [sb("xsqt0", [128, 512], BF16),
                     sb("xsqt1", [128, 512], BF16)]
        T["tab"] = sb("tab", [128, 1024], BF16)
        T["gout"] = sb("gout", [128, 512], BF16)
        T["extmask"] = sb("extmask", [128, 512], BF16)
        T["gtm32"] = sb("gtm32", [128, 128], F32)
        T["gq"] = sb("gq", [128, 4, 512], BF16)
        T["vtm_sb"] = [sb("vtm0", [128, 512], BF16)]
        T["idxu"] = sb("idxu", [128, 4, 32], U16)

        T["gtm_bf"] = sb("gtm_bf", [128, 128], BF16)
        T["e_bf"] = sb("e_bf", [32, 512], BF16)
        T["R_bf"] = sb("R_bf", [8, 512], BF16)
        T["bcol2"] = [sb("bcol2a", [128, CH, 1], BF16),
                      sb("bcol2b", [128, CH, 1], BF16)]
        T["bcol4"] = [sb("bcol4a", [128, CH, 1], BF16),
                      sb("bcol4b", [128, CH, 1], BF16)]

        def ps(name, shape, dt=F32):
            return es.enter_context(nc.psum_tensor(name, shape, dt))[:]

        T["psA"] = ps("psA", [128, 512])
        T["psB"] = ps("psB", [128, 512])
        T["psSa"] = ps("psSa", [128, 1024])
        T["psSb"] = ps("psSb", [128, 1024])
        T["psSc"] = ps("psSc", [128, 1024])
        T["psc_bf"] = T["psSa"][:, 448:512].bitcast(BF16)
        T["psc_v"] = T["psSc"].bitcast(BF16)

        T["const_loads"] = [
            (T["idb"], T["idb_d"]), (T["idf"], T["idf_d"]),
            (T["ones_col"], T["onec_d"]), (T["ones_mean"], T["onem_d"]),
            (T["ones_row"], T["oner_d"]),
            (T["blkd8"], T["blkd8_d"]), (T["blkdT8"], T["blkdT8_d"]),
            (T["bcat"], T["bcat_d"]),
            (T["cvals"], T["cvals_d"]),
            (T["v_bout"], T["bout_d"]), (T["v_boutb"], T["boutb_d"]),
            (T["v_mur"], T["mur_d"]), (T["v_muk"], T["muk_d"]),
        ]

        def wview(dr):
            return dr.rearrange("(a p) d -> p a d", p=128)

        T["castw_loads_p0"] = [
            (T["wq"], wview(T["wq_d"])), (T["wkv"], wview(T["wkv_d"])),
        ]
        T["castw_loads_mid"] = [
            (T["wout"], wview(T["wout_d"])),
            (T["wcat"], wview(T["wcat_d"])),
        ]
        T["castw_loads_late"] = [
            ("wrT", T["wrT"], wview(T["wrT_d"])),
            ("wkT", T["wkT"], wview(T["wkT_d"])),
            ("wvT", T["wvT"], wview(T["wvT_d"])),
            ("em", T["extmask"], T["extmask_d"]),
        ]

        semnames = ["s_cst", "s_cstw", "s_w2", "s_wb", "s_wrT", "s_wkT",
                    "s_wvT", "s_em", "s_x1", "s_x2", "s_x3a", "s_x3b",
                    "s_ao3a", "s_ao3b",
                    "s_sao0", "s_sao1", "s_x0", "s_y0", "s_y1",
                    "s_vt0", "s_vt1", "s_vl0", "s_vl1", "s_x1s", "s_r2a", "s_r2b",
                    "pe", "act", "dve", "pool"]
        semh = {nm: es.enter_context(nc.semaphore(nm)) for nm in semnames}

        block = es.enter_context(nc.Block())
        for engname in ("gpsimd", "scalar", "vector", "tensor", "sync"):
            def runner(eng, engname=engname):
                sems = {nm: Sem(nm) for nm in semnames}
                for nm in semnames:
                    sems[nm].h = semh[nm]
                engines = dict.fromkeys(
                    ("gpsimd", "scalar", "vector", "tensor", "sync"))
                engines[engname] = eng
                schedule(Prog(engines, sems, engname), T, dbg)
            getattr(block, engname)(runner)
    return nc


def host_inputs(inp, b):
    x = np.asarray(inp["L0_attn_a_hidden_states"])
    m = {}
    m["xT"] = np.ascontiguousarray(x[b].T.astype(np.float32))
    m["wq_d"] = np.ascontiguousarray(np.asarray(inp["L0_attn_a_Wq"],
                                                np.float32))
    m["wkv_d"] = np.ascontiguousarray(np.asarray(inp["L0_attn_a_Wkv"],
                                                 np.float32))
    m["wout_d"] = np.ascontiguousarray(np.asarray(inp["L0_attn_a_Wout"],
                                                  np.float32))
    m["wrT_d"] = np.ascontiguousarray(
        np.asarray(inp["L0_ffn_W_r"], np.float32).T)
    m["wkT_d"] = np.ascontiguousarray(
        np.asarray(inp["L0_ffn_W_k"], np.float32).T)
    m["wvT_d"] = np.ascontiguousarray(
        np.asarray(inp["L0_ffn_W_v"], np.float32).T)
    m["woutb_d"] = np.ascontiguousarray(
        np.asarray(inp["L0_attn_b_W_out"], np.float32))
    woff = np.asarray(inp["L0_attn_b_W_off"], np.float32)
    watt = np.asarray(inp["L0_attn_b_W_attn"], np.float32)
    m["wcat_d"] = np.ascontiguousarray(
        np.concatenate([woff[:, 0::2], watt], axis=1))
    boff = np.asarray(inp["L0_attn_b_b_off"], np.float32)
    batt = np.asarray(inp["L0_attn_b_b_attn"], np.float32)
    m["bcat_d"] = np.ascontiguousarray(
        np.concatenate([boff[0::2], batt]).reshape(64, 1))

    def fm(v):
        return np.ascontiguousarray(
            np.asarray(v, np.float32).reshape(CH, 128).T)

    m["bout_d"] = fm(inp["L0_attn_a_bout"])
    m["boutb_d"] = fm(inp["L0_attn_b_b_out"])
    m["mur_d"] = fm(inp["L0_ffn_mu_r"])
    m["muk_d"] = fm(inp["L0_ffn_mu_k"])
    m["idb_d"] = np.eye(128).astype(_bf)
    m["idf_d"] = np.eye(32).astype(np.float32)
    m["onec_d"] = np.ones((128, 1), _bf)
    m["onem_d"] = np.full((128, 1), 1.0 / 1024.0, _bf)
    m["oner_d"] = np.ones((1, 128), _bf)
    blk = np.zeros((32, 8), np.float32)
    for h in range(8):
        blk[4 * h:4 * h + 4, h] = 1.0
    m["blkd8_d"] = blk.astype(_bf)
    m["blkdT8_d"] = np.ascontiguousarray(blk.T).astype(_bf)
    em = np.zeros((128, 512), np.float32)
    for p in range(128):
        for h in range(32):
            em[p, 16 * h + (p % 16)] = 1.0
    m["extmask_d"] = em.astype(_bf)
    cv = np.zeros((128, 2), np.float32)
    cv[:, 0] = EPS
    cv[:, 1] = 1.0
    m["cvals_d"] = cv
    return m


def kernel(**inputs):
    B = 8
    nc = build_nc(dbg=0)
    in_maps = [host_inputs(inputs, b) for b in range(B)]
    res = run_bass_kernel_spmd(nc, in_maps, core_ids=list(range(B)))
    out = np.empty((B, S, D), np.float32)
    for b in range(B):
        out[b] = np.asarray(res.results[b]["yT"]).T
    return out


# version 43
# speedup vs baseline: 1.0294x; 1.0294x over previous
import numpy as np
import ml_dtypes
from contextlib import ExitStack

import concourse.bass as bass
import concourse.mybir as mybir
from concourse.bass_utils import run_bass_kernel_spmd

F32 = mybir.dt.float32
BF16 = mybir.dt.bfloat16
U16 = mybir.dt.uint16
I32 = mybir.dt.int32
U8 = mybir.dt.uint8
AF = mybir.ActivationFunctionType
ALU = mybir.AluOpType

S = 4096
D = 1024
CH = 8
HP = 8
TB1 = 256
NT1 = S // TB1
SB = 512
NSB = S // SB
TB3 = 512
NT3 = S // TB3
EPS = 1e-5
KB = 1024
AW = 180 * KB

_bf = ml_dtypes.bfloat16


class Sem:
    __slots__ = ("name", "h", "n")

    def __init__(self, name):
        self.name = name
        self.h = None
        self.n = 0


class Prog:
    def __init__(self, engines, sems, active):
        self.engines = engines
        self.sems = sems
        self.active = active

    def _go(self, eng, method, args, kwargs, waits, sem, k):
        s = self.sems[sem]
        real = self.engines[eng] if eng == self.active else None
        for w in waits:
            if w is None:
                continue
            ws, wv = w
            wsem = self.sems[ws]
            assert 0 <= wv <= wsem.n, (ws, wv, wsem.n, method)
            if real is not None and wv > 0:
                real.wait_ge(wsem.h, wv)
        if real is not None:
            getattr(real, method)(*args, **kwargs).then_inc(s.h, k)
        s.n += k
        return s.n

    def t(self, method, *a, waits=(), **kw):
        return self._go("tensor", method, a, kw, waits, "pe", 1)

    def v(self, method, *a, waits=(), **kw):
        return self._go("vector", method, a, kw, waits, "dve", 1)

    def s(self, method, *a, waits=(), **kw):
        return self._go("scalar", method, a, kw, waits, "act", 1)

    def g(self, method, *a, waits=(), **kw):
        return self._go("gpsimd", method, a, kw, waits, "pool", 1)

    def dma(self, eng, out, in_, sem, waits=(), **kw):
        return self._go(eng, "dma_start", (out, in_), kw, waits, sem, 16)

    def wait(self, eng, *waits):
        real = self.engines[eng] if eng == self.active else None
        for ws, wv in waits:
            wsem = self.sems[ws]
            assert wv <= wsem.n, (ws, wv, wsem.n, "standalone")
            if real is not None:
                real.wait_ge(wsem.h, wv)


def schedule(P, T, dbg):
    pe, act, dve, pool = "pe", "act", "dve", "pool"

    cst = 0
    for dst, src in T["const_loads"]:
        cst = P.dma("sync", dst, src, "s_cst")
    cstw = 0
    for dst, src in T["castw_loads_p0"]:
        cstw = P.dma("gpsimd", dst, src, "s_cstw")
    n_cst = cst
    CST = ("s_cst", n_cst)
    CSTW = ("s_cstw", cstw)

    xT = T["xT"]
    wq, wkv = T["wq"], T["wkv"]
    Q, KV = T["Q"], T["KV"]
    LNx1 = T["LNx1"]
    onec, oner = T["ones_col"], T["ones_row"]
    onem = T["ones_mean"]
    psA, psB, psSa, psSb = T["psA"], T["psB"], T["psSa"], T["psSb"]
    psSc = T["psSc"]
    psc_bf = T["psc_bf"]
    rows_bf = T["rows_bf"]
    rows_f = T["rows_f"]

    lnst = {"xb": [0, 0], "xq": [0, 0]}

    def emit_ln_stats(xsrc, N, xwaits, ps_s1, ps_s2, ps_rs, ps_nm,
                      cast_pool=False):
        xbt, xsqt = T["xbt"], T["xsqt"]
        mm1 = mm2 = 0
        for c in range(CH):
            b = c & 1
            cw = list(xwaits)
            if lnst["xb"][b] > 0:
                cw.append((pe, lnst["xb"][b]))
            if cast_pool:
                mcast = P.g("tensor_copy", xbt[b][:, 0:N], xsrc[:, c, :],
                            waits=cw)
            else:
                mcast = P.v("tensor_copy", xbt[b][:, 0:N], xsrc[:, c, :],
                            waits=cw)
            sw = list(xwaits)
            if lnst["xq"][b] > 0:
                sw.append((pe, lnst["xq"][b]))
            msq = P.s("activation", xsqt[b][:, 0:N], xsrc[:, c, :],
                      AF.Square, waits=sw)
            w1 = [(pool if cast_pool else dve, mcast)]
            if c == 0:
                w1.append(CST)
            mm1 = P.t("matmul", ps_s1[0:1, 0:N], onem, xbt[b][:, 0:N],
                      start=(c == 0), stop=(c == CH - 1), waits=w1)
            lnst["xb"][b] = mm1
            mm2 = P.t("matmul", ps_s2[0:1, 0:N], onem, xsqt[b][:, 0:N],
                      start=(c == 0), stop=(c == CH - 1),
                      waits=[(act, msq)])
            lnst["xq"][b] = mm2
        m_f = rows_f[:, 0:N]
        q_f = rows_f[:, 512:512 + N]
        v_f = rows_f[:, 512 + N:512 + 2 * N]
        mk = P.v("tensor_copy", m_f, ps_s1[0:1, 0:N], waits=[(pe, mm1)])
        mk = P.v("tensor_tensor", v_f, m_f, m_f, ALU.mult, waits=[(dve, mk)])
        mk = P.v("tensor_tensor", v_f, ps_s2[0:1, 0:N], v_f, ALU.subtract,
                 waits=[(dve, mk), (pe, mm2)])
        ma = P.s("activation", q_f, v_f, AF.Ln, bias=T["cvals"][0:1, 0:1],
                 waits=[(dve, mk)])
        ma = P.s("activation", v_f, q_f, AF.Exp, scale=-0.5,
                 waits=[(act, ma)])
        mk = P.v("scalar_tensor_tensor", q_f, m_f, -1.0, v_f,
                 ALU.mult, ALU.mult, waits=[(act, ma), (dve, mk)])
        mk = P.v("tensor_copy", rows_bf[:, 0:2 * N],
                 rows_f[:, 512:512 + 2 * N], waits=[(dve, mk)])
        return mk

    def emit_ln_bcast(N, rows_mk, ps_rs, ps_nm):
        nm_b = rows_bf[:, 0:N]
        rs_b = rows_bf[:, N:2 * N]
        P.t("matmul", ps_rs[:, 0:N], oner, rs_b, start=True, stop=True,
            waits=[(dve, rows_mk), CST])
        mmn = P.t("matmul", ps_nm[:, 0:N], oner, nm_b, start=True,
                  stop=True)
        return mmn

    xf1 = T["xf1"]
    ld_sems = ["s_x1", "s_x2"]
    ld_mark = [0, 0]
    xf_free = [0, 0]
    evq_m = [0, 0]
    evk_m = [0, 0]
    vtm_ev = [0, 0]
    vtm_ev_d = [0, 0]
    vtm_st = [0, 0]
    lnx1_free = 0

    rows_m = [0] * NT1
    apply_m = [0, 0]

    def p1_load(t):
        b = t & 1
        w = []
        if apply_m[b] > 0:
            w.append((dve, apply_m[b]))
        ld_mark[b] = P.dma("sync", xf1[b],
                           xT.rearrange("(a p) t -> p a t", p=128)
                           [:, :, t * TB1:(t + 1) * TB1],
                           ld_sems[b], waits=w)

    def p1_stats(t):
        b = t & 1
        rows_m[t] = emit_ln_stats(xf1[b], TB1,
                                  [(ld_sems[b], ld_mark[b])],
                                  psSa[0:1, 512:1024],
                                  psSb[0:1, 512:1024], None, None)

    for t in range(NT1):
        if t == 0:
            p1_load(0)
            p1_load(1)
            p1_stats(0)
        b = t & 1
        bcm = emit_ln_bcast(TB1, rows_m[t], psSa[:, 512:768],
                            psSb[:, 512:768])
        rsB = psSa[:, 512:768].unsqueeze(1).broadcast_to([128, CH, TB1])
        nmB = psSb[:, 512:768].unsqueeze(1).broadcast_to([128, CH, TB1])
        mk = P.v("tensor_tensor", xf1[b], xf1[b], rsB, ALU.mult,
                 waits=[(pe, bcm), (ld_sems[b], ld_mark[b])])
        lw = [(dve, mk)]
        if lnx1_free > 0:
            lw.append((pe, lnx1_free))
        mk = P.v("tensor_tensor", LNx1, xf1[b], nmB, ALU.add, waits=lw)
        apply_m[b] = mk
        if t + 2 < NT1:
            p1_load(t + 2)
        if t + 1 < NT1:
            p1_stats(t + 1)
        for dj in range(CH):
            pq = dj & 1
            psq = (psA, psB)[pq]
            w = [(dve, mk), CST, CSTW]
            if evq_m[pq] > 0:
                w.append((act, evq_m[pq]))
            mmq = 0
            for c in range(CH):
                mmq = P.t("matmul", psq[:, 0:TB1],
                          wq[:, c, dj * 128:(dj + 1) * 128],
                          LNx1[:, c, :], start=(c == 0), stop=(c == CH - 1),
                          waits=(w if c == 0 else ()))
            evq_m[pq] = P.s("activation", Q[:, dj, t * TB1:(t + 1) * TB1],
                            psq[:, 0:TB1], AF.Copy, scale=0.125,
                            waits=[(pe, mmq)])
            psk = (psSa, psSb)[pq]
            w2 = [(dve, mk)]
            if evk_m[pq] > 0:
                w2.append((dve, evk_m[pq]))
            mmk = 0
            for c in range(CH):
                mmk = P.t("matmul", psk[:, 0:TB1],
                          wkv[:, c, dj * 128:(dj + 1) * 128],
                          LNx1[:, c, :], start=(c == 0), stop=(c == CH - 1),
                          waits=(w2 if c == 0 else ()))
            evk_m[pq] = P.v("tensor_copy", KV[:, dj, t * TB1:(t + 1) * TB1],
                            psk[:, 0:TB1], waits=[(pe, mmk)])
            lnx1_free = mmk
        psc_v = T["psc_v"]
        for tch in range(TB1 // 128):
            tok0 = (t * 2 + tch) * 128
            mtv = 0
            for hp in range(HP):
                w = []
                if hp == 0:
                    w = [(dve, evk_m[0]), (dve, evk_m[1])]
                    for pq2 in range(2):
                        if vtm_ev_d[pq2]:
                            w.append((dve, vtm_ev_d[pq2]))
                mtv = P.t("transpose", psc_v[:, hp * 128:(hp + 1) * 128],
                          KV[:, hp, tok0:tok0 + 128], T["idb"], waits=w)
            for half in range(2):
                pq = (tch * 2 + half) & 1
                vtm_ev_d[pq] = P.v("tensor_copy", T["vtm_sb"][pq][:, :],
                                   psc_v[:, half * 512:(half + 1) * 512],
                                   waits=[(pe, mtv)] + (
                                       [("s_vt%d" % pq, vtm_st[pq])]
                                       if vtm_st[pq] else []))
                vtm_st[pq] = P.dma(
                    "sync", T["d_vtm"][tok0:tok0 + 128,
                                       half * 512:(half + 1) * 512],
                    T["vtm_sb"][pq][:, :], "s_vt%d" % pq,
                    waits=[(dve, vtm_ev_d[pq])])

    p1_pe = P.sems["pe"].n
    p1_act = P.sems["act"].n
    p1_dve = P.sems["dve"].n

    if dbg == 1:
        dq = P.dma("sync", T["dbg_q"], Q[:, :, :], "s_y0",
                   waits=[(act, p1_act)])
        dk = P.dma("sync", T["dbg_kv"], KV[:, :, :], "s_y1",
                   waits=[(dve, p1_dve)])
        P.wait("sync", ("s_y0", dq), ("s_y1", dk))
        return

    mw = 0
    for dst, src in T["castw_loads_mid"]:
        mw = P.dma("gpsimd", dst, src, "s_w2",
                   waits=[(act, p1_act), (dve, p1_dve), (pe, p1_pe)])
    n_w2 = mw

    V_tm = T["V_tm"]
    sr = T["sr"]
    ao_sb = T["ao_sb"]
    d_ao = T["d_ao"]
    idb = T["idb"]

    av_prev_hp = 0
    sr_free = [0] * 3
    psS_free = [None] * 3
    ao_free = [0, 0]
    sao_sems = ["s_sao0", "s_sao1"]
    sao_cnt = [0, 0]
    av_ev_a = av_ev_b = 0
    vl_sems = ["s_vl0", "s_vl1"]
    vl_cnt = [0, 0]
    vt_rend = [("s_vt0", P.sems["s_vt0"].n), ("s_vt1", P.sems["s_vt1"].n)]

    def vtm_view(h):
        return T["d_vtm"].rearrange("(tc p) d -> p tc d", p=128)[
            :, :, h * 128:(h + 1) * 128]

    for hp in range(HP):
        if hp == 0:
            vl_cnt[0] = P.dma("sync", V_tm[0][:, :, :], vtm_view(0),
                              "s_vl0", waits=vt_rend + [(pe, p1_pe)])
            vl_cnt[1] = P.dma("sync", V_tm[1][:, :, :], vtm_view(1),
                              "s_vl1", waits=vt_rend + [(pe, p1_pe)])
        elif hp < HP - 1:
            hb = (hp + 1) & 1
            vl_cnt[hb] = P.dma("sync", V_tm[hb][:, :, :], vtm_view(hp + 1),
                               vl_sems[hb], waits=[(pe, av_prev_hp)])
        vtm = V_tm[hp & 1]
        vl_mark = (vl_sems[hp & 1], vl_cnt[hp & 1])

        for sb in range(NSB):
            sslc = slice(sb * SB, (sb + 1) * SB)
            av_mm = 0
            ev_marks = [None] * 32

            def emit_qkt(tc):
                pp = tc % 3
                psS = (psSa, psSb, psSc)[pp]
                w = []
                if psS_free[pp] is not None:
                    w.extend(psS_free[pp])
                if tc < 3:
                    if hp == 0 and sb == 0:
                        w.append((act, p1_act))
                        w.append((dve, p1_dve))
                        w.append((pe, p1_pe))
                P.t("matmul", psS[:, 0:512],
                    KV[0:64, hp, tc * 128:(tc + 1) * 128],
                    Q[0:64, hp, sslc], start=True, stop=True,
                    tile_position=(0, 0), waits=w)
                mmb = P.t("matmul", psS[:, 512:1024],
                          KV[64:128, hp, tc * 128:(tc + 1) * 128],
                          Q[64:128, hp, sslc], start=True, stop=True,
                          tile_position=(64, 0))
                ew = [(pe, mmb)]
                if sr_free[pp] > 0:
                    ew.append((pe, sr_free[pp]))
                mva = P.v("tensor_scalar", sr[pp][:, 0:512], psS[:, 0:512],
                          0.0, None, ALU.max, waits=ew)
                mvb = P.s("activation", sr[pp][:, 512:1024],
                          psS[:, 512:1024], AF.Relu, waits=ew)
                ev_marks[tc] = (mva, mvb)
                psS_free[pp] = [(dve, mva), (act, mvb)]

            def emit_av(tc):
                nonlocal av_mm
                pp = tc % 3
                mva, mvb = ev_marks[tc]
                w2 = [(dve, mva), (act, mvb)]
                if tc == 0:
                    w2.append(vl_mark)
                    if av_ev_a:
                        w2.append((dve, av_ev_a))
                    if av_ev_b:
                        w2.append((act, av_ev_b))
                P.t("matmul", psA[0:64, :], vtm[:, tc, 0:64],
                    sr[pp][:, 0:512], start=(tc == 0), stop=(tc == 31),
                    tile_position=(0, 0), waits=w2)
                av_mm = P.t("matmul", psB[64:128, :], vtm[:, tc, 64:128],
                            sr[pp][:, 512:1024], start=(tc == 0),
                            stop=(tc == 31), tile_position=(0, 64))
                sr_free[pp] = av_mm

            emit_qkt(0)
            emit_qkt(1)
            for tc in range(2, 32):
                emit_qkt(tc)
                emit_av(tc - 2)
            emit_av(30)
            emit_av(31)
            pq = sb & 1
            ew = [(pe, av_mm)]
            if ao_free[pq] > 0:
                ew.append((sao_sems[pq], ao_free[pq]))
            av_ev_a = P.v("tensor_copy", ao_sb[pq][0:64, :], psA[0:64, :],
                          waits=ew)
            av_ev_b = P.s("copy", ao_sb[pq][64:128, :],
                          psB[64:128, :], waits=ew)
            sao_cnt[pq] = P.dma("sync",
                                d_ao[hp * 128:(hp + 1) * 128, sslc],
                                ao_sb[pq][:, :], sao_sems[pq],
                                waits=[(dve, av_ev_a), (act, av_ev_b)])
            ao_free[pq] = sao_cnt[pq]
        av_prev_hp = av_mm

    attn_pe = P.sems["pe"].n
    attn_dve = P.sems["dve"].n
    attn_act = P.sems["act"].n

    woutb = T["woutb"]
    colsum_b = T["colsum_b"]
    nwb = P.dma("gpsimd", woutb, T["woutb_d"].rearrange(
        "(a p) d -> p a d", p=128), "s_wb", waits=[(pe, attn_pe)])
    mcs = 0
    cm = 0
    for half in range(2):
        dst = (psSa, psSb)[half]
        for c in range(CH):
            w = ()
            if c == 0:
                w = [("s_wb", nwb), (dve, attn_dve), (act, attn_act)]
            cm = P.t("matmul", dst[0:1, 0:512], onec,
                     woutb[:, c, half * 512:(half + 1) * 512],
                     start=(c == 0), stop=(c == CH - 1), waits=w)
        mcs = P.v("tensor_copy", colsum_b[:, half * 512:(half + 1) * 512],
                  dst[0:1, 0:512],
                  waits=[(pe, cm)] + ([(dve, mcs)] if mcs else []))

    w3 = {}
    for (name, dst, src) in T["castw_loads_late"]:
        w3[name] = P.dma("gpsimd", dst, src, "s_" + name,
                         waits=[(pe, attn_pe), (pe, cm)])

    if dbg == 2:
        dq = P.dma("sync", T["dbg_cs"], colsum_b[:, :], "s_y0",
                   waits=[(dve, mcs)])
        P.wait("sync", ("s_y0", dq),
               ("s_sao0", sao_cnt[0]), ("s_sao1", sao_cnt[1]))
        return

    aot = T["aot"]
    xf3 = T["xf3"]
    d_x1 = T["d_x1"]
    wout = T["wout"]
    v_bout = T["v_bout"]
    evp = [("dve", attn_dve), ("act", attn_act)]
    evpC = [psS_free[2][0], psS_free[2][1]]
    banksA = (T["psA"][:, 0:TB3], T["psB"][:, 0:TB3])
    banksC = (T["psSc"][:, 0:512], T["psSc"][:, 512:1024])
    aotb = [aot, T["aot2"]]
    ao_sems = ["s_ao3a", "s_ao3b"]
    ao_lds = [0, 0]
    aot_freem = [0, 0]
    x1_st = [0, 0]
    x1_ev = [0, 0]
    ld25 = [0, 0]
    n_r2 = 0

    def ao_load(sb):
        ab = sb & 1
        aw = [("s_sao0", sao_cnt[0]), ("s_sao1", sao_cnt[1]),
              (pe, attn_pe)]
        if aot_freem[ab] > 0:
            aw.append((pe, aot_freem[ab]))
        ao_lds[ab] = P.dma("gpsimd", aotb[ab],
                           d_ao.rearrange("(a p) t -> p a t", p=128)
                           [:, :, sb * SB:(sb + 1) * SB],
                           ao_sems[ab], waits=aw)

    for sb in range(NSB):
        b = sb & 1
        sslc = slice(sb * SB, (sb + 1) * SB)
        if sb == 0:
            ao_load(0)
            ao_load(1)
        elif sb + 1 < NSB:
            ao_load(sb + 1)
        w = []
        if x1_st[b] > 0:
            w.append((("s_y0", "s_y1")[b], x1_st[b]))
        ld25[b] = P.dma("sync", xf3[b],
                        xT.rearrange("(a p) t -> p a t", p=128)[:, :, sslc],
                        ("s_x3a", "s_x3b")[b], waits=w)
        last = 0
        for dj in range(CH):
            pq = dj & 1
            psq = (psA, psB)[pq]
            w = [(ao_sems[b], ao_lds[b]), ("s_w2", n_w2),
                 (("s_x3a", "s_x3b")[b], ld25[b]), evp[pq]]
            mm = 0
            for c in range(CH):
                mm = P.t("matmul", psq[:, 0:SB],
                         wout[:, c, dj * 128:(dj + 1) * 128],
                         aotb[b][:, c, :], start=(c == 0),
                         stop=(c == CH - 1),
                         waits=(w if c == 0 else ()))
            if dj == CH - 1:
                aot_freem[b] = mm
            last = P.v("scalar_tensor_tensor", xf3[b][:, dj, :],
                       psq[:, 0:SB], v_bout[:, dj:dj + 1], xf3[b][:, dj, :],
                       ALU.add, ALU.add, waits=[(pe, mm)])
            evp[pq] = (dve, last)
        x1_ev[b] = last
        x1_st[b] = P.dma("sync", d_x1.rearrange("(a p) t -> p a t", p=128)
                         [:, :, sslc], xf3[b], ("s_y0", "s_y1")[b],
                         waits=[(dve, last)])
        rw = [(dve, last)]
        if sb > 0:
            rw.append(("s_x1s", n_r2))
        rmk = emit_ln_stats(xf3[b], SB, rw,
                            psSa[0:1, 512:1024], psSb[0:1, 512:1024],
                            None, None)
        n_r2 = P.dma("sync", T["d_r2"][0:1, sb * 1024:(sb + 1) * 1024],
                     rows_bf[:, 0:1024], "s_x1s", waits=[(dve, rmk)])
        rows_free = rmk
    x1_rend = [("s_y0", x1_st[0]), ("s_y1", x1_st[1])]
    r2_rend = ("s_x1s", n_r2)

    LNxp = T["LNx"]
    wrT, wkT, wvT = T["wrT"], T["wkT"], T["wvT"]
    wcat = T["wcat"]
    bcat = T["bcat"]
    v_boutb = T["v_boutb"]
    v_mur, v_muk = T["v_mur"], T["v_muk"]
    bcol2, bcol4 = T["bcol2"], T["bcol4"]
    idf = T["idf"]
    tab = T["tab"]
    gout = T["gout"]
    idxu = T["idxu"]
    extmask = T["extmask"]
    blkd8, blkdT8 = T["blkd8"], T["blkdT8"]
    x0row = T["x0row"]
    d_x0 = T["d_x0"]
    gtm_bf = T["gtm_bf"]
    gfm = T["gfm"]
    e_bf, R_bf, gw_bf, s_bf = T["e_bf"], T["R_bf"], T["gw_bf"], T["s_bf"]
    ta, tb_, tc2, td, tneg = T["ta"], T["tb"], T["tc2"], T["td"], T["tneg"]
    cint = T["cint"]
    yT = T["yT"]

    mz = 0
    for bc_t in (bcol2[0], bcol2[1], bcol4[0], bcol4[1]):
        mz = P.v("memset", bc_t, 0.0, waits=[(dve, attn_dve)])
    mz_done = mz

    ld3_sems = ["s_x3a", "s_x3b"]
    ld3_mark = list(ld25)
    xf3_free = list(x1_st)
    y_sems = ["s_y0", "s_y1"]
    y_cnt = list(x1_st)
    sa1 = ("dve", mcs)
    sb1 = ("dve", mcs)
    sc1 = ("act", attn_act)
    sc2 = ("act", attn_act)
    tab_ready = 0
    st8 = {}

    r2_sems = ["s_r2a", "s_r2b"]
    r2_mark = [0, 0]
    r2_free = [0, 0]

    def S_loads(t):
        b = t & 1
        w = list(x1_rend)
        if xf3_free[b] > 0:
            w.append((y_sems[b], xf3_free[b]))
        ld3_mark[b] = P.dma("sync", xf3[b],
                            d_x1.rearrange("(a p) t -> p a t", p=128)
                            [:, :, t * TB3:(t + 1) * TB3],
                            ld3_sems[b], waits=w)
        rw = [r2_rend]
        if r2_free[b] > 0:
            rw.append((pe, r2_free[b]))
        r2_mark[b] = P.dma("gpsimd", T["rows2"][b],
                           T["d_r2"][0:1, t * 1024:(t + 1) * 1024],
                           r2_sems[b], waits=rw)

    def emit_ln(t, xwaits):
        x = xf3[t & 1]
        lnx = LNxp[t & 1]
        rmk = emit_ln_stats(x, TB3, xwaits,
                            psSa[0:1, 512:1024], psSb[0:1, 512:1024],
                            None, None, cast_pool=False)
        bc = emit_ln_bcast(TB3, rmk, psSa[:, 512:1024],
                           psSb[:, 512:1024])
        rsB = psSa[:, 512:1024].unsqueeze(1).broadcast_to([128, CH, TB3])
        nmB = psSb[:, 512:1024].unsqueeze(1).broadcast_to([128, CH, TB3])
        mk = P.v("tensor_tensor", lnx, x, rsB, ALU.mult,
                 waits=[(pe, bc), (dve, mz_done)] + xwaits)
        mk = P.v("tensor_tensor", lnx, lnx, nmB, ALU.add,
                 waits=[(dve, mk)])
        return mk

    def emit_ffn(t, ln_m, bc_rd, bc_wr, xin_m, banks, evp):
        b = t & 1
        x = xf3[b]
        lnx = LNxp[b]
        xr, xk = T["xr"][b], T["xk"][b]
        kr, r_bf16 = T["kr"][b], T["r_bf16"][b]
        mt = P.v("tensor_copy", bc_wr, lnx[:, :, TB3 - 1:TB3],
                 waits=[(dve, ln_m)])
        md = P.v("tensor_tensor", kr[:, :, 1:TB3], lnx[:, :, 1:TB3],
                 lnx[:, :, 0:TB3 - 1], ALU.subtract, waits=[(dve, mt)])
        md = P.v("tensor_tensor", kr[:, :, 0:1], lnx[:, :, 0:1],
                 bc_rd, ALU.subtract, waits=[(dve, md)])
        msc = 0
        mxr = md
        for c in range(CH):
            ew = [(dve, md)]
            if msc:
                ew.append((act, msc))
            msc = P.s("activation", xr[:, c, 1:TB3], kr[:, c, 1:TB3],
                      AF.Copy, scale=v_mur[:, c:c + 1], waits=ew)
            mxr = P.v("tensor_tensor", xr[:, c, 1:TB3], xr[:, c, 1:TB3],
                      lnx[:, c, 0:TB3 - 1], ALU.add,
                      waits=[(act, msc), (dve, mxr)])
            mxr = P.v("scalar_tensor_tensor", xr[:, c, 0:1],
                      kr[:, c, 0:1], v_mur[:, c:c + 1],
                      bc_rd[:, c, :], ALU.mult, ALU.add,
                      waits=[(dve, mxr)])
        mxk = mxr
        for c in range(CH):
            msc = P.s("activation", xk[:, c, 1:TB3], kr[:, c, 1:TB3],
                      AF.Copy, scale=v_muk[:, c:c + 1],
                      waits=[(act, msc), (dve, md)])
            mxk = P.v("tensor_tensor", xk[:, c, 1:TB3], xk[:, c, 1:TB3],
                      lnx[:, c, 0:TB3 - 1], ALU.add,
                      waits=[(act, msc), (dve, mxk)])
            mxk = P.v("scalar_tensor_tensor", xk[:, c, 0:1],
                      kr[:, c, 0:1], v_muk[:, c:c + 1],
                      bc_rd[:, c, :], ALU.mult, ALU.add,
                      waits=[(dve, mxk)])
        mex = 0
        for dj in range(CH):
            pq = dj & 1
            w = [(dve, mxr), ("s_wrT", w3["wrT"]), evp[pq]]
            mm = 0
            for c in range(CH):
                mm = P.t("matmul", banks[pq],
                         wrT[:, c, dj * 128:(dj + 1) * 128],
                         xr[:, c, :], start=(c == 0), stop=(c == CH - 1),
                         waits=(w if c == 0 else ()))
            ew = [(pe, mm)]
            if mex:
                ew.append((act, mex))
            mex = P.s("activation", r_bf16[:, dj, :],
                      banks[pq], AF.Exp, scale=-1.0, waits=ew)
            evp[pq] = (act, mex)
        mkr = 0
        for dj in range(CH):
            pq = dj & 1
            w = [(dve, mxk), ("s_wkT", w3["wkT"]), evp[pq]]
            mm = 0
            for c in range(CH):
                mm = P.t("matmul", banks[pq],
                         wkT[:, c, dj * 128:(dj + 1) * 128],
                         xk[:, c, :], start=(c == 0), stop=(c == CH - 1),
                         waits=(w if c == 0 else ()))
            mkr = P.s("activation", kr[:, dj, :],
                      banks[pq], AF.Relu,
                      waits=[(pe, mm), (dve, mxk) if dj == 0 else None])
            evp[pq] = (act, mkr)
        mln = P.s("activation", r_bf16, r_bf16, AF.Ln,
                  bias=T["cvals"][:, 1:2], waits=[(act, mkr)])
        mr = P.s("activation", r_bf16, r_bf16, AF.Exp, scale=-1.0,
                 waits=[(act, mln)])
        mka = P.v("tensor_tensor", kr, kr, kr, ALU.mult,
                  waits=[(act, mkr)])
        mfin = 0
        for dj in range(CH):
            pq = dj & 1
            w = [(dve, mka), ("s_wvT", w3["wvT"]), evp[pq]]
            mm = 0
            for c in range(CH):
                mm = P.t("matmul", banks[pq],
                         wvT[:, c, dj * 128:(dj + 1) * 128],
                         kr[:, c, :], start=(c == 0), stop=(c == CH - 1),
                         waits=(w if c == 0 else ()))
            mt1 = P.v("scalar_tensor_tensor", xr[:, dj, :],
                      r_bf16[:, dj, :], 1.0, banks[pq],
                      ALU.mult, ALU.mult,
                      waits=[(pe, mm), (act, mr), (dve, xin_m),
                             (dve, mka)])
            mfin = P.v("tensor_tensor", x[:, dj, :], x[:, dj, :],
                       xr[:, dj, :], ALU.add, waits=[(dve, mt1)])
            evp[pq] = (dve, mfin)
        return mfin

    def S5a(t):
        nonlocal sa1, sc1, tab_ready
        b = t & 1
        lnx = LNxp[b]
        ln3 = st8[(t, "ln3")]
        if t == 0:
            mrow = 0
            for a in range(CH):
                wv = [(dve, ln3), sc1] if a == 0 else [(dve, mrow)]
                mtp = P.t("transpose", psc_bf[0:1, 0:128],
                          lnx[:, a, 0:1], idb, waits=wv)
                mrow = P.v("tensor_copy", x0row[:, a * 128:(a + 1) * 128],
                           psc_bf[0:1, 0:128], waits=[(pe, mtp)])
            sc1 = ("dve", mrow)
            m1 = P.dma("sync", d_x0, x0row[:, :], "s_x0",
                       waits=[(dve, mrow)])
            tab_ready = P.dma("sync", tab[:, :],
                              d_x0[0:1, :].broadcast_to([128, 1024]),
                              "s_x0", waits=[("s_x0", m1)])
        w = [(dve, ln3), ("s_w2", n_w2), sa1]
        if t == 0:
            w.append(sc1)
        mm = 0
        for c in range(CH):
            mm = P.t("matmul", psSa[0:64, 0:512], wcat[:, c, :],
                     lnx[:, c, :], start=(c == 0), stop=(c == CH - 1),
                     waits=(w if c == 0 else ()))
        moff = P.v("tensor_scalar", ta[0:32, :], psSa[0:32, 0:512],
                   bcat[0:32, :], None, ALU.add, waits=[(pe, mm)])
        me = P.s("activation", tb_[0:32, :], psSa[32:64, 0:512], AF.Exp,
                 bias=bcat[32:64, :], waits=[(pe, mm)])
        st8[(t, "sa1r")] = [(dve, moff), (act, me)]
        mk = P.v("tensor_copy", cint[0:32, :], ta[0:32, :],
                 waits=[(dve, moff)])
        mk = P.v("tensor_copy", tc2[0:32, :], cint[0:32, :],
                 waits=[(dve, mk)])
        mk = P.v("tensor_tensor", td[0:32, :], tc2[0:32, :], ta[0:32, :],
                 ALU.is_gt, waits=[(dve, mk)])
        mk = P.v("tensor_tensor", tc2[0:32, :], tc2[0:32, :], td[0:32, :],
                 ALU.subtract, waits=[(dve, mk)])
        mk = P.v("tensor_tensor", td[0:32, :], tc2[0:32, :], ta[0:32, :],
                 ALU.not_equal, waits=[(dve, mk)])
        mk = P.v("tensor_scalar", tneg[0:32, :], ta[0:32, :], 0.0, None,
                 ALU.is_lt, waits=[(dve, mk)])
        mk = P.v("tensor_tensor", td[0:32, :], td[0:32, :], tneg[0:32, :],
                 ALU.mult, waits=[(dve, mk)])
        mk = P.v("tensor_tensor", ta[0:32, :], tc2[0:32, :], td[0:32, :],
                 ALU.add, waits=[(dve, mk)])
        mk = P.v("tensor_scalar", td[0:32, :], ta[0:32, :], 0.0, None,
                 ALU.is_lt, waits=[(dve, mk)])
        mk = P.v("scalar_tensor_tensor", ta[0:32, :], td[0:32, :], 1024.0,
                 ta[0:32, :], ALU.mult, ALU.add, waits=[(dve, mk)])
        midx = P.v("tensor_scalar", ta[0:32, :], ta[0:32, :], 1023.0, 0.0,
                   ALU.min, ALU.max, waits=[(dve, mk)])
        nonlocal sc2
        mtp_prev = 0
        tp_free = [sc1, sc2]
        for q in range(4):
            bsl = (psSa[:, 64:96], psSb[:, 64:96])[q & 1]
            wv = [tp_free[q & 1]]
            if q == 0:
                wv.append((dve, midx))
                wv.append(st8[(t, "sa1r")][1])
            mtp = P.t("transpose", bsl,
                      ta[0:32, q * 128:(q + 1) * 128], idf[:, :],
                      waits=wv)
            mtp_prev = P.v("tensor_copy", idxu[:, q, :], bsl,
                           waits=[(pe, mtp)] + ([(dve, mtp_prev)]
                                                if mtp_prev else []))
            tp_free[q & 1] = (dve, mtp_prev)
        sc1 = ("dve", mtp_prev)
        sc2 = tp_free[1]
        nonlocal sb1
        me = st8[(t, "sa1r")][1]
        meb = P.v("tensor_copy", e_bf[0:32, :], tb_[0:32, :], waits=[me])
        mS = P.t("matmul", psSb[0:8, 0:512], blkd8, e_bf[0:32, :],
                 start=True, stop=True, waits=[(dve, meb), sb1])
        mlnS = P.s("activation", td[0:8, :], psSb[0:8, 0:512], AF.Ln,
                   waits=[(pe, mS), (dve, midx)])
        mR = P.s("activation", tc2[0:8, :], td[0:8, :], AF.Exp, scale=-1.0,
                 waits=[(act, mlnS), (dve, mtp_prev)])
        mRb = P.v("tensor_copy", R_bf[0:8, :], tc2[0:8, :],
                  waits=[(act, mR)])
        mRB = P.t("matmul", psSb[0:32, 0:512], blkdT8, R_bf[0:8, :],
                  start=True, stop=True, waits=[(dve, mRb), (act, mlnS)])
        sb1 = ("dve", mRb)
        st8[(t, "meb")] = meb
        st8[(t, "mR")] = mR
        st8[(t, "mRB")] = mRB
        gq = T["gq"]
        mmul = 0
        for q in range(4):
            wv = [(dve, mtp_prev), ("s_x0", tab_ready)]
            if mmul:
                wv.append((pool, mmul))
            mgth = P.g("indirect_copy", gout[:, :], tab[:, :],
                       idxu[:, q, :], True, waits=wv)
            mmul = P.g("tensor_tensor", gq[:, q, :], gout[:, :],
                       extmask[:, :], ALU.mult,
                       waits=[(pool, mgth), ("s_em", w3["em"])])
        st8[(t, "gmul")] = mmul

    def S5b(t):
        nonlocal sa1, sb1, sc1, sc2, evp
        b = t & 1
        x = xf3[b]
        mred = 0
        for q in range(4):
            mred = P.v("tensor_reduce", T["gtm32"][:, q * 32:(q + 1) * 32],
                       T["gq"][:, q, :].rearrange("p (h q) -> p h q", q=16),
                       mybir.AxisListType.X, ALU.add,
                       waits=[(pool, st8[(t, "gmul")])] +
                             ([(dve, mred)] if mred else []))
        gcast = P.v("tensor_copy", gtm_bf[:, 0:128], T["gtm32"][:, 0:128],
                    waits=[(dve, mred)])
        ffn1_m = st8[(t, "ffn1")]
        mgf = 0
        for q in range(4):
            wv = [(dve, gcast), sc1]
            if mgf:
                wv.append((dve, mgf))
            mtp = P.t("transpose", psc_bf[0:32, 0:128],
                      gtm_bf[:, q * 32:(q + 1) * 32], idb, waits=wv)
            mgf = P.v("tensor_copy", gfm[:, q * 128:(q + 1) * 128],
                      psc_bf[0:32, 0:128], waits=[(pe, mtp)])
        sc1 = ("dve", mgf)
        meb = st8[(t, "meb")]
        mRB = st8[(t, "mRB")]
        mR = st8[(t, "mR")]
        mk = P.v("tensor_tensor", td[0:32, :], gfm[:, :], e_bf[0:32, :],
                 ALU.mult, waits=[(dve, mgf), (dve, meb), (act, mR)])
        mk = P.v("tensor_tensor", gw_bf[0:32, :], td[0:32, :],
                 psSb[0:32, 0:512], ALU.mult, waits=[(dve, mk), (pe, mRB)])
        ms = P.t("matmul", psSb[0:1, 0:512], onec[0:32, :],
                 gw_bf[0:32, :], start=True, stop=True, waits=[(dve, mk)])
        msb = P.v("tensor_copy", s_bf[:, :], psSb[0:1, 0:512],
                  waits=[(pe, ms)])
        sb1 = ("dve", msb)
        mab = 0
        for dj in range(CH):
            po = dj & 1
            dst = (psSa[:, 0:512], psSb[:, 0:512])[po]
            w = [(dve, msb), (dve, mcs), (sa1 if po == 0 else sc2), CST]
            if dj < 2:
                w.extend(st8[(t, "sa1r")])
            mm = P.t("matmul", dst, colsum_b[:, dj * 128:(dj + 1) * 128],
                     s_bf[:, :], start=True, stop=True, waits=w)
            mab = P.v("scalar_tensor_tensor", x[:, dj, :], dst,
                      v_boutb[:, dj:dj + 1], x[:, dj, :], ALU.add, ALU.add,
                      waits=[(pe, mm), (dve, ffn1_m)])
            if po == 0:
                sa1 = ("dve", mab)
            else:
                sc2 = ("dve", mab)
        st8[(t, "mab")] = mab

    def S8(t):
        b = t & 1
        y_cnt[b] = P.dma("sync",
                         yT.rearrange("(a p) t -> p a t", p=128)
                         [:, :, t * TB3:(t + 1) * TB3],
                         xf3[b], y_sems[b],
                         waits=[(dve, st8[(t, "ffn2")])])
        xf3_free[b] = y_cnt[b]

    def emit_ln2(t, prev_mk):
        b = t & 1
        x = xf3[b]
        lnx = LNxp[b]
        rw = [(r2_sems[b], r2_mark[b]), (dve, mz_done)]
        if prev_mk is not None:
            rw.append((dve, prev_mk))
        P.t("matmul", psSa[:, 512:1024], oner, T["rows2"][b][:, 512:1024],
            start=True, stop=True, waits=rw + [CST])
        bc = P.t("matmul", psSb[:, 512:1024], oner,
                 T["rows2"][b][:, 0:512], start=True, stop=True)
        r2_free[b] = bc
        rsB = psSa[:, 512:1024].unsqueeze(1).broadcast_to([128, CH, TB3])
        nmB = psSb[:, 512:1024].unsqueeze(1).broadcast_to([128, CH, TB3])
        mk = P.v("tensor_tensor", lnx, x, rsB, ALU.mult,
                 waits=[(pe, bc), (ld3_sems[b], ld3_mark[b]),
                        (dve, mz_done)])
        mk = P.v("tensor_tensor", lnx, lnx, nmB, ALU.add,
                 waits=[(dve, mk)])
        return mk

    S_loads(0)
    S_loads(1)
    st8[(0, "ln2")] = emit_ln2(0, rows_free)
    st8[(0, "ffn1")] = emit_ffn(0, st8[(0, "ln2")], bcol2[0], bcol2[1],
                                st8[(0, "ln2")], banksA, evp)
    st8[(0, "ln3")] = emit_ln(0, [(dve, st8[(0, "ffn1")])])
    S5a(0)
    for t in range(1, NT3):
        st8[(t, "ln2")] = emit_ln2(t, st8[(t - 1, "ln3")])
        st8[(t, "ffn1")] = emit_ffn(t, st8[(t, "ln2")], bcol2[t & 1],
                                    bcol2[(t + 1) & 1], st8[(t, "ln2")],
                                    banksA, evp)
        S5b(t - 1)
        st8[(t - 1, "ln4")] = emit_ln(t - 1, [(dve, st8[(t - 1, "mab")])])
        st8[(t - 1, "ffn2")] = emit_ffn(t - 1, st8[(t - 1, "ln4")],
                                        bcol4[(t - 1) & 1], bcol4[t & 1],
                                        st8[(t - 1, "mab")], banksC, evpC)
        S8(t - 1)
        if t + 1 < NT3:
            S_loads(t + 1)
        st8[(t, "ln3")] = emit_ln(t, [(dve, st8[(t, "ffn1")])])
        S5a(t)
    tl = NT3 - 1
    S5b(tl)
    st8[(tl, "ln4")] = emit_ln(tl, [(dve, st8[(tl, "mab")])])
    st8[(tl, "ffn2")] = emit_ffn(tl, st8[(tl, "ln4")],
                                 bcol4[tl & 1], bcol4[(tl + 1) & 1],
                                 st8[(tl, "mab")], banksC, evpC)
    S8(tl)

    P.wait("sync", ("s_y0", y_cnt[0]), ("s_y1", y_cnt[1]))


def build_nc(dbg=0):
    nc = bass.Bass("TRN2", target_bir_lowering=False)
    T = {}

    def din(name, shape, dt=F32):
        return nc.dram_tensor(name, shape, dt, kind="ExternalInput").ap()

    T["xT"] = din("xT", [D, S])
    for nm in ("wq_d", "wkv_d", "wout_d", "wrT_d", "wkT_d", "wvT_d",
               "woutb_d"):
        T[nm] = din(nm, [D, D])
    T["wcat_d"] = din("wcat_d", [D, 64])
    T["bcat_d"] = din("bcat_d", [64, 1])
    T["bout_d"] = din("bout_d", [128, CH])
    T["boutb_d"] = din("boutb_d", [128, CH])
    T["mur_d"] = din("mur_d", [128, CH])
    T["muk_d"] = din("muk_d", [128, CH])
    T["idb_d"] = din("idb_d", [128, 128], BF16)
    T["idf_d"] = din("idf_d", [32, 32], F32)
    T["onec_d"] = din("onec_d", [128, 1], BF16)
    T["onem_d"] = din("onem_d", [128, 1], BF16)
    T["oner_d"] = din("oner_d", [1, 128], BF16)
    T["blkd8_d"] = din("blkd8_d", [32, 8], BF16)
    T["blkdT8_d"] = din("blkdT8_d", [8, 32], BF16)
    T["extmask_d"] = din("extmask_d", [128, 512], BF16)
    T["cvals_d"] = din("cvals_d", [128, 2], F32)

    T["yT"] = nc.dram_tensor("yT", [D, S], F32, kind="ExternalOutput").ap()
    T["d_ao"] = nc.dram_tensor("d_ao", [D, S], BF16,
                               kind=("ExternalOutput" if dbg == 2 else "Internal")).ap()
    T["d_x0"] = nc.dram_tensor("d_x0", [1, D], BF16, kind="Internal").ap()
    T["d_vtm"] = nc.dram_tensor("d_vtm", [S, D], BF16, kind="Internal").ap()
    T["d_x1"] = nc.dram_tensor("d_x1", [D, S], F32, kind="Internal").ap()
    T["d_r2"] = nc.dram_tensor("d_r2", [1, 2 * S], BF16, kind="Internal").ap()
    if dbg == 1:
        T["dbg_q"] = nc.dram_tensor("dbg_q", [128, CH, S], BF16,
                                    kind="ExternalOutput").ap()
        T["dbg_kv"] = nc.dram_tensor("dbg_kv", [128, CH, S], BF16,
                                     kind="ExternalOutput").ap()
    if dbg == 2:
        T["dbg_cs"] = nc.dram_tensor("dbg_cs", [1, D], BF16,
                                     kind="ExternalOutput").ap()

    es = ExitStack()
    with es:
        arena = es.enter_context(nc.sbuf_tensor("arena", [128, AW], U8))
        ar = arena[:]

        def carve(off_kb, shape, dt):
            isz = {F32: 4, BF16: 2, U16: 2, I32: 4}[dt]
            n = 1
            for d in shape[1:]:
                n *= d
            off = int(off_kb * KB)
            assert off + n * isz <= AW, (off_kb, shape)
            v = ar[:, off:off + n * isz].bitcast(dt)
            if len(shape) == 3:
                v = v.rearrange("p (a n) -> p a n", a=shape[1])
            return v

        T["Q"] = carve(0, [128, CH, S], BF16)
        T["xf3"] = [carve(0, [128, CH, TB3], F32),
                    carve(16, [128, CH, TB3], F32)]
        T["LNx"] = [carve(32, [128, CH, TB3], BF16),
                    carve(40, [128, CH, TB3], BF16)]
        T["xr"] = [carve(48, [128, CH, TB3], BF16),
                   carve(56, [128, CH, TB3], BF16)]
        T["KV"] = carve(64, [128, CH, S], BF16)
        T["wrT"] = carve(64, [128, CH, D], BF16)
        T["wkT"] = carve(80, [128, CH, D], BF16)
        T["wvT"] = carve(96, [128, CH, D], BF16)
        T["xk"] = [carve(112, [128, CH, TB3], BF16),
                    carve(120, [128, CH, TB3], BF16)]
        T["aot"] = carve(112, [128, CH, TB3], BF16)
        T["aot2"] = carve(120, [128, CH, TB3], BF16)
        T["wq"] = carve(128, [128, CH, D], BF16)
        T["wkv"] = carve(144, [128, CH, D], BF16)
        T["xf1"] = [carve(160, [128, CH, TB1], F32),
                    carve(168, [128, CH, TB1], F32)]
        T["LNx1"] = carve(176, [128, CH, TB1], BF16)
        T["V_tm"] = [carve(128, [128, 32, 128], BF16),
                     carve(136, [128, 32, 128], BF16)]
        T["sr"] = [carve(144, [128, 1024], BF16),
                   carve(146, [128, 1024], BF16),
                   carve(148, [128, 1024], BF16)]
        T["ao_sb"] = [carve(150, [128, 512], BF16),
                      carve(151, [128, 512], BF16)]
        T["wout"] = carve(152, [128, CH, D], BF16)
        T["wcat"] = carve(172, [128, CH, 64], BF16)
        T["woutb"] = carve(64, [128, CH, D], BF16)
        T["kr"] = [carve(128, [128, CH, TB3], BF16),
                   carve(136, [128, CH, TB3], BF16)]
        T["r_bf16"] = [carve(144, [128, CH, TB3], BF16),
                       carve(152, [128, CH, TB3], BF16)]
        T["ta"] = carve(160, [128, TB3], F32)
        T["tb"] = carve(162, [128, TB3], F32)
        T["tc2"] = carve(164, [128, TB3], F32)
        T["td"] = carve(166, [128, TB3], F32)
        T["cint"] = carve(166, [128, TB3], I32)
        T["tneg"] = carve(168, [128, TB3], BF16)
        T["s_bf"] = carve(169, [128, 512], BF16)[0:1, :]
        T["colsum_b"] = carve(178, [128, 1024], BF16)[0:1, :]
        T["gfm"] = carve(173, [128, 512], BF16)[0:32, :]
        T["gw_bf"] = T["gfm"]
        T["x0row"] = carve(170, [128, 1024], BF16)[0:1, :]
        T["rows2"] = [carve(174, [128, 1024], BF16)[0:1, :],
                      carve(176, [128, 1024], BF16)[0:1, :]]

        def sb(name, shape, dt):
            return es.enter_context(nc.sbuf_tensor(name, shape, dt))[:]

        T["idb"] = sb("idb", [128, 128], BF16)
        T["idf"] = sb("idf", [32, 32], F32)
        T["ones_col"] = sb("ones_col", [128, 1], BF16)
        T["ones_mean"] = sb("ones_mean", [128, 1], BF16)
        T["ones_row"] = sb("ones_row", [1, 128], BF16)
        T["blkd8"] = sb("blkd8", [32, 8], BF16)
        T["blkdT8"] = sb("blkdT8", [8, 32], BF16)
        T["cvals"] = sb("cvals", [128, 2], F32)
        T["bcat"] = sb("bcat", [64, 1], F32)
        T["v_bout"] = sb("v_bout", [128, CH], F32)
        T["v_boutb"] = sb("v_boutb", [128, CH], F32)
        T["v_mur"] = sb("v_mur", [128, CH], F32)
        T["v_muk"] = sb("v_muk", [128, CH], F32)
        T["rows_bf"] = sb("rows_bf", [1, 1024], BF16)
        T["rows_f"] = sb("rows_f", [1, 1536], F32)
        T["xbt"] = [sb("xbt0", [128, 512], BF16),
                    sb("xbt1", [128, 512], BF16)]
        T["xsqt"] = [sb("xsqt0", [128, 512], BF16),
                     sb("xsqt1", [128, 512], BF16)]
        T["tab"] = sb("tab", [128, 1024], BF16)
        T["gout"] = sb("gout", [128, 512], BF16)
        T["extmask"] = sb("extmask", [128, 512], BF16)
        T["gtm32"] = sb("gtm32", [128, 128], F32)
        T["gq"] = sb("gq", [128, 4, 512], BF16)
        T["gout2"] = sb("gout2", [128, 512], BF16)
        T["vtm_sb"] = [sb("vtm0", [128, 512], BF16), sb("vtm1", [128, 512], BF16)]
        T["idxu"] = sb("idxu", [128, 4, 32], U16)

        T["gtm_bf"] = sb("gtm_bf", [128, 128], BF16)
        T["e_bf"] = sb("e_bf", [32, 512], BF16)
        T["R_bf"] = sb("R_bf", [8, 512], BF16)
        T["bcol2"] = [sb("bcol2a", [128, CH, 1], BF16),
                      sb("bcol2b", [128, CH, 1], BF16)]
        T["bcol4"] = [sb("bcol4a", [128, CH, 1], BF16),
                      sb("bcol4b", [128, CH, 1], BF16)]

        def ps(name, shape, dt=F32):
            return es.enter_context(nc.psum_tensor(name, shape, dt))[:]

        T["psA"] = ps("psA", [128, 512])
        T["psB"] = ps("psB", [128, 512])
        T["psSa"] = ps("psSa", [128, 1024])
        T["psSb"] = ps("psSb", [128, 1024])
        T["psSc"] = ps("psSc", [128, 1024])
        T["psc_bf"] = T["psSa"][:, 448:512].bitcast(BF16)
        T["psc_v"] = T["psSc"].bitcast(BF16)

        T["const_loads"] = [
            (T["idb"], T["idb_d"]), (T["idf"], T["idf_d"]),
            (T["ones_col"], T["onec_d"]), (T["ones_mean"], T["onem_d"]),
            (T["ones_row"], T["oner_d"]),
            (T["blkd8"], T["blkd8_d"]), (T["blkdT8"], T["blkdT8_d"]),
            (T["bcat"], T["bcat_d"]),
            (T["cvals"], T["cvals_d"]),
            (T["v_bout"], T["bout_d"]), (T["v_boutb"], T["boutb_d"]),
            (T["v_mur"], T["mur_d"]), (T["v_muk"], T["muk_d"]),
        ]

        def wview(dr):
            return dr.rearrange("(a p) d -> p a d", p=128)

        T["castw_loads_p0"] = [
            (T["wq"], wview(T["wq_d"])), (T["wkv"], wview(T["wkv_d"])),
        ]
        T["castw_loads_mid"] = [
            (T["wout"], wview(T["wout_d"])),
            (T["wcat"], wview(T["wcat_d"])),
        ]
        T["castw_loads_late"] = [
            ("wrT", T["wrT"], wview(T["wrT_d"])),
            ("wkT", T["wkT"], wview(T["wkT_d"])),
            ("wvT", T["wvT"], wview(T["wvT_d"])),
            ("em", T["extmask"], T["extmask_d"]),
        ]

        semnames = ["s_cst", "s_cstw", "s_w2", "s_wb", "s_wrT", "s_wkT",
                    "s_wvT", "s_em", "s_x1", "s_x2", "s_x3a", "s_x3b",
                    "s_ao3a", "s_ao3b",
                    "s_sao0", "s_sao1", "s_x0", "s_y0", "s_y1",
                    "s_vt0", "s_vt1", "s_vl0", "s_vl1", "s_x1s", "s_r2a", "s_r2b",
                    "pe", "act", "dve", "pool"]
        semh = {nm: es.enter_context(nc.semaphore(nm)) for nm in semnames}

        block = es.enter_context(nc.Block())
        for engname in ("gpsimd", "scalar", "vector", "tensor", "sync"):
            def runner(eng, engname=engname):
                sems = {nm: Sem(nm) for nm in semnames}
                for nm in semnames:
                    sems[nm].h = semh[nm]
                engines = dict.fromkeys(
                    ("gpsimd", "scalar", "vector", "tensor", "sync"))
                engines[engname] = eng
                schedule(Prog(engines, sems, engname), T, dbg)
            getattr(block, engname)(runner)
    return nc


def host_inputs(inp, b):
    x = np.asarray(inp["L0_attn_a_hidden_states"])
    m = {}
    m["xT"] = np.ascontiguousarray(x[b].T.astype(np.float32))
    m["wq_d"] = np.ascontiguousarray(np.asarray(inp["L0_attn_a_Wq"],
                                                np.float32))
    m["wkv_d"] = np.ascontiguousarray(np.asarray(inp["L0_attn_a_Wkv"],
                                                 np.float32))
    m["wout_d"] = np.ascontiguousarray(np.asarray(inp["L0_attn_a_Wout"],
                                                  np.float32))
    m["wrT_d"] = np.ascontiguousarray(
        np.asarray(inp["L0_ffn_W_r"], np.float32).T)
    m["wkT_d"] = np.ascontiguousarray(
        np.asarray(inp["L0_ffn_W_k"], np.float32).T)
    m["wvT_d"] = np.ascontiguousarray(
        np.asarray(inp["L0_ffn_W_v"], np.float32).T)
    m["woutb_d"] = np.ascontiguousarray(
        np.asarray(inp["L0_attn_b_W_out"], np.float32))
    woff = np.asarray(inp["L0_attn_b_W_off"], np.float32)
    watt = np.asarray(inp["L0_attn_b_W_attn"], np.float32)
    m["wcat_d"] = np.ascontiguousarray(
        np.concatenate([woff[:, 0::2], watt], axis=1))
    boff = np.asarray(inp["L0_attn_b_b_off"], np.float32)
    batt = np.asarray(inp["L0_attn_b_b_attn"], np.float32)
    m["bcat_d"] = np.ascontiguousarray(
        np.concatenate([boff[0::2], batt]).reshape(64, 1))

    def fm(v):
        return np.ascontiguousarray(
            np.asarray(v, np.float32).reshape(CH, 128).T)

    m["bout_d"] = fm(inp["L0_attn_a_bout"])
    m["boutb_d"] = fm(inp["L0_attn_b_b_out"])
    m["mur_d"] = fm(inp["L0_ffn_mu_r"])
    m["muk_d"] = fm(inp["L0_ffn_mu_k"])
    m["idb_d"] = np.eye(128).astype(_bf)
    m["idf_d"] = np.eye(32).astype(np.float32)
    m["onec_d"] = np.ones((128, 1), _bf)
    m["onem_d"] = np.full((128, 1), 1.0 / 1024.0, _bf)
    m["oner_d"] = np.ones((1, 128), _bf)
    blk = np.zeros((32, 8), np.float32)
    for h in range(8):
        blk[4 * h:4 * h + 4, h] = 1.0
    m["blkd8_d"] = blk.astype(_bf)
    m["blkdT8_d"] = np.ascontiguousarray(blk.T).astype(_bf)
    em = np.zeros((128, 512), np.float32)
    for p in range(128):
        for h in range(32):
            em[p, 16 * h + (p % 16)] = 1.0
    m["extmask_d"] = em.astype(_bf)
    cv = np.zeros((128, 2), np.float32)
    cv[:, 0] = EPS
    cv[:, 1] = 1.0
    m["cvals_d"] = cv
    return m


def kernel(**inputs):
    B = 8
    nc = build_nc(dbg=0)
    in_maps = [host_inputs(inputs, b) for b in range(B)]
    res = run_bass_kernel_spmd(nc, in_maps, core_ids=list(range(B)))
    out = np.empty((B, S, D), np.float32)
    for b in range(B):
        out[b] = np.asarray(res.results[b]["yT"]).T
    return out


# version 44
# speedup vs baseline: 1.0447x; 1.0149x over previous
import numpy as np
import ml_dtypes
from contextlib import ExitStack

import concourse.bass as bass
import concourse.mybir as mybir
from concourse.bass_utils import run_bass_kernel_spmd

F32 = mybir.dt.float32
BF16 = mybir.dt.bfloat16
U16 = mybir.dt.uint16
I32 = mybir.dt.int32
U8 = mybir.dt.uint8
AF = mybir.ActivationFunctionType
ALU = mybir.AluOpType

S = 4096
D = 1024
CH = 8
HP = 8
TB1 = 256
NT1 = S // TB1
SB = 512
NSB = S // SB
TB3 = 512
NT3 = S // TB3
EPS = 1e-5
KB = 1024
AW = 180 * KB

_bf = ml_dtypes.bfloat16


class Sem:
    __slots__ = ("name", "h", "n")

    def __init__(self, name):
        self.name = name
        self.h = None
        self.n = 0


class Prog:
    def __init__(self, engines, sems, active):
        self.engines = engines
        self.sems = sems
        self.active = active

    def _go(self, eng, method, args, kwargs, waits, sem, k):
        s = self.sems[sem]
        real = self.engines[eng] if eng == self.active else None
        for w in waits:
            if w is None:
                continue
            ws, wv = w
            wsem = self.sems[ws]
            assert 0 <= wv <= wsem.n, (ws, wv, wsem.n, method)
            if real is not None and wv > 0:
                real.wait_ge(wsem.h, wv)
        if real is not None:
            getattr(real, method)(*args, **kwargs).then_inc(s.h, k)
        s.n += k
        return s.n

    def t(self, method, *a, waits=(), **kw):
        return self._go("tensor", method, a, kw, waits, "pe", 1)

    def v(self, method, *a, waits=(), **kw):
        return self._go("vector", method, a, kw, waits, "dve", 1)

    def s(self, method, *a, waits=(), **kw):
        return self._go("scalar", method, a, kw, waits, "act", 1)

    def g(self, method, *a, waits=(), **kw):
        return self._go("gpsimd", method, a, kw, waits, "pool", 1)

    def dma(self, eng, out, in_, sem, waits=(), **kw):
        return self._go(eng, "dma_start", (out, in_), kw, waits, sem, 16)

    def wait(self, eng, *waits):
        real = self.engines[eng] if eng == self.active else None
        for ws, wv in waits:
            wsem = self.sems[ws]
            assert wv <= wsem.n, (ws, wv, wsem.n, "standalone")
            if real is not None:
                real.wait_ge(wsem.h, wv)


def schedule(P, T, dbg):
    pe, act, dve, pool = "pe", "act", "dve", "pool"

    cst = 0
    for dst, src in T["const_loads"]:
        cst = P.dma("sync", dst, src, "s_cst")
    cstw = 0
    for dst, src in T["castw_loads_p0"]:
        cstw = P.dma("gpsimd", dst, src, "s_cstw")
    n_cst = cst
    CST = ("s_cst", n_cst)
    CSTW = ("s_cstw", cstw)

    xT = T["xT"]
    wq, wkv = T["wq"], T["wkv"]
    Q, KV = T["Q"], T["KV"]
    LNx1 = T["LNx1"]
    onec, oner = T["ones_col"], T["ones_row"]
    onem = T["ones_mean"]
    psA, psB, psSa, psSb = T["psA"], T["psB"], T["psSa"], T["psSb"]
    psSc = T["psSc"]
    psc_bf = T["psc_bf"]
    rows_bf = T["rows_bf"]
    rows_f = T["rows_f"]

    lnst = {"xb": [0, 0], "xq": [0, 0]}

    def emit_ln_stats(xsrc, N, xwaits, ps_s1, ps_s2, ps_rs, ps_nm,
                      cast_pool=False):
        xbt, xsqt = T["xbt"], T["xsqt"]
        mm1 = mm2 = 0
        for c in range(CH):
            b = c & 1
            cw = list(xwaits)
            if lnst["xb"][b] > 0:
                cw.append((pe, lnst["xb"][b]))
            if cast_pool:
                mcast = P.g("tensor_copy", xbt[b][:, 0:N], xsrc[:, c, :],
                            waits=cw)
            else:
                mcast = P.v("tensor_copy", xbt[b][:, 0:N], xsrc[:, c, :],
                            waits=cw)
            sw = list(xwaits)
            if lnst["xq"][b] > 0:
                sw.append((pe, lnst["xq"][b]))
            msq = P.s("activation", xsqt[b][:, 0:N], xsrc[:, c, :],
                      AF.Square, waits=sw)
            w1 = [(pool if cast_pool else dve, mcast)]
            if c == 0:
                w1.append(CST)
            mm1 = P.t("matmul", ps_s1[0:1, 0:N], onem, xbt[b][:, 0:N],
                      start=(c == 0), stop=(c == CH - 1), waits=w1)
            lnst["xb"][b] = mm1
            mm2 = P.t("matmul", ps_s2[0:1, 0:N], onem, xsqt[b][:, 0:N],
                      start=(c == 0), stop=(c == CH - 1),
                      waits=[(act, msq)])
            lnst["xq"][b] = mm2
        m_f = rows_f[:, 0:N]
        q_f = rows_f[:, 512:512 + N]
        v_f = rows_f[:, 512 + N:512 + 2 * N]
        mk = P.v("tensor_copy", m_f, ps_s1[0:1, 0:N], waits=[(pe, mm1)])
        mk = P.v("tensor_tensor", v_f, m_f, m_f, ALU.mult, waits=[(dve, mk)])
        mk = P.v("tensor_tensor", v_f, ps_s2[0:1, 0:N], v_f, ALU.subtract,
                 waits=[(dve, mk), (pe, mm2)])
        ma = P.s("activation", q_f, v_f, AF.Ln, bias=T["cvals"][0:1, 0:1],
                 waits=[(dve, mk)])
        ma = P.s("activation", v_f, q_f, AF.Exp, scale=-0.5,
                 waits=[(act, ma)])
        mk = P.v("scalar_tensor_tensor", q_f, m_f, -1.0, v_f,
                 ALU.mult, ALU.mult, waits=[(act, ma), (dve, mk)])
        mk = P.v("tensor_copy", rows_bf[:, 0:2 * N],
                 rows_f[:, 512:512 + 2 * N], waits=[(dve, mk)])
        return mk

    def emit_ln_bcast(N, rows_mk, ps_rs, ps_nm):
        nm_b = rows_bf[:, 0:N]
        rs_b = rows_bf[:, N:2 * N]
        P.t("matmul", ps_rs[:, 0:N], oner, rs_b, start=True, stop=True,
            waits=[(dve, rows_mk), CST])
        mmn = P.t("matmul", ps_nm[:, 0:N], oner, nm_b, start=True,
                  stop=True)
        return mmn

    xf1 = T["xf1"]
    ld_sems = ["s_x1", "s_x2"]
    ld_mark = [0, 0]
    xf_free = [0, 0]
    evq_m = [0, 0]
    evk_m = [0, 0]
    vtm_ev = [0, 0]
    vtm_ev_d = [0, 0]
    vtm_st = [0, 0]
    lnx1_free = 0

    rows_m = [0] * NT1
    apply_m = [0, 0]

    def p1_load(t):
        b = t & 1
        w = []
        if apply_m[b] > 0:
            w.append((dve, apply_m[b]))
        ld_mark[b] = P.dma("sync", xf1[b],
                           xT.rearrange("(a p) t -> p a t", p=128)
                           [:, :, t * TB1:(t + 1) * TB1],
                           ld_sems[b], waits=w)

    def p1_stats(t):
        b = t & 1
        rows_m[t] = emit_ln_stats(xf1[b], TB1,
                                  [(ld_sems[b], ld_mark[b])],
                                  psSa[0:1, 512:1024],
                                  psSb[0:1, 512:1024], None, None)

    for t in range(NT1):
        if t == 0:
            p1_load(0)
            p1_load(1)
            p1_stats(0)
        b = t & 1
        bcm = emit_ln_bcast(TB1, rows_m[t], psSa[:, 512:768],
                            psSb[:, 512:768])
        rsB = psSa[:, 512:768].unsqueeze(1).broadcast_to([128, CH, TB1])
        nmB = psSb[:, 512:768].unsqueeze(1).broadcast_to([128, CH, TB1])
        mk = P.v("tensor_tensor", xf1[b], xf1[b], rsB, ALU.mult,
                 waits=[(pe, bcm), (ld_sems[b], ld_mark[b])])
        lw = [(dve, mk)]
        if lnx1_free > 0:
            lw.append((pe, lnx1_free))
        mk = P.v("tensor_tensor", LNx1, xf1[b], nmB, ALU.add, waits=lw)
        apply_m[b] = mk
        if t + 2 < NT1:
            p1_load(t + 2)
        if t + 1 < NT1:
            p1_stats(t + 1)
        for dj in range(CH):
            pq = dj & 1
            psq = (psA, psB)[pq]
            w = [(dve, mk), CST, CSTW]
            if evq_m[pq] > 0:
                w.append((act, evq_m[pq]))
            mmq = 0
            for c in range(CH):
                mmq = P.t("matmul", psq[:, 0:TB1],
                          wq[:, c, dj * 128:(dj + 1) * 128],
                          LNx1[:, c, :], start=(c == 0), stop=(c == CH - 1),
                          waits=(w if c == 0 else ()))
            evq_m[pq] = P.s("activation", Q[:, dj, t * TB1:(t + 1) * TB1],
                            psq[:, 0:TB1], AF.Copy, scale=0.125,
                            waits=[(pe, mmq)])
            psk = (psSa, psSb)[pq]
            w2 = [(dve, mk)]
            if evk_m[pq] > 0:
                w2.append((dve, evk_m[pq]))
            mmk = 0
            for c in range(CH):
                mmk = P.t("matmul", psk[:, 0:TB1],
                          wkv[:, c, dj * 128:(dj + 1) * 128],
                          LNx1[:, c, :], start=(c == 0), stop=(c == CH - 1),
                          waits=(w2 if c == 0 else ()))
            evk_m[pq] = P.v("tensor_copy", KV[:, dj, t * TB1:(t + 1) * TB1],
                            psk[:, 0:TB1], waits=[(pe, mmk)])
            lnx1_free = mmk
        psc_v = T["psc_v"]
        for tch in range(TB1 // 128):
            tok0 = (t * 2 + tch) * 128
            mtv = 0
            for hp in range(HP):
                w = []
                if hp == 0:
                    w = [(dve, evk_m[0]), (dve, evk_m[1])]
                    for pq2 in range(2):
                        if vtm_ev_d[pq2]:
                            w.append((dve, vtm_ev_d[pq2]))
                mtv = P.t("transpose", psc_v[:, hp * 128:(hp + 1) * 128],
                          KV[:, hp, tok0:tok0 + 128], T["idb"], waits=w)
            for half in range(2):
                pq = (tch * 2 + half) & 1
                vtm_ev_d[pq] = P.v("tensor_copy", T["vtm_sb"][pq][:, :],
                                   psc_v[:, half * 512:(half + 1) * 512],
                                   waits=[(pe, mtv)] + (
                                       [("s_vt%d" % pq, vtm_st[pq])]
                                       if vtm_st[pq] else []))
                vtm_st[pq] = P.dma(
                    "sync", T["d_vtm"][tok0:tok0 + 128,
                                       half * 512:(half + 1) * 512],
                    T["vtm_sb"][pq][:, :], "s_vt%d" % pq,
                    waits=[(dve, vtm_ev_d[pq])])

    p1_pe = P.sems["pe"].n
    p1_act = P.sems["act"].n
    p1_dve = P.sems["dve"].n

    if dbg == 1:
        dq = P.dma("sync", T["dbg_q"], Q[:, :, :], "s_y0",
                   waits=[(act, p1_act)])
        dk = P.dma("sync", T["dbg_kv"], KV[:, :, :], "s_y1",
                   waits=[(dve, p1_dve)])
        P.wait("sync", ("s_y0", dq), ("s_y1", dk))
        return

    mw = 0
    for dst, src in T["castw_loads_mid"]:
        mw = P.dma("gpsimd", dst, src, "s_w2",
                   waits=[(act, p1_act), (dve, p1_dve), (pe, p1_pe)])
    n_w2 = mw

    V_tm = T["V_tm"]
    sr = T["sr"]
    ao_sb = T["ao_sb"]
    d_ao = T["d_ao"]
    idb = T["idb"]

    av_prev_hp = 0
    sr_free = [0] * 3
    psS_free = [None] * 3
    ao_free = [0, 0]
    sao_sems = ["s_sao0", "s_sao1"]
    sao_cnt = [0, 0]
    av_ev_a = av_ev_b = 0
    vl_sems = ["s_vl0", "s_vl1"]
    vl_cnt = [0, 0]
    vt_rend = [("s_vt0", P.sems["s_vt0"].n), ("s_vt1", P.sems["s_vt1"].n)]

    def vtm_view(h):
        return T["d_vtm"].rearrange("(tc p) d -> p tc d", p=128)[
            :, :, h * 128:(h + 1) * 128]

    for hp in range(HP):
        if hp == 0:
            vl_cnt[0] = P.dma("sync", V_tm[0][:, :, :], vtm_view(0),
                              "s_vl0", waits=vt_rend + [(pe, p1_pe)])
            vl_cnt[1] = P.dma("sync", V_tm[1][:, :, :], vtm_view(1),
                              "s_vl1", waits=vt_rend + [(pe, p1_pe)])
        elif hp < HP - 1:
            hb = (hp + 1) & 1
            vl_cnt[hb] = P.dma("sync", V_tm[hb][:, :, :], vtm_view(hp + 1),
                               vl_sems[hb], waits=[(pe, av_prev_hp)])
        vtm = V_tm[hp & 1]
        vl_mark = (vl_sems[hp & 1], vl_cnt[hp & 1])

        for sb in range(NSB):
            sslc = slice(sb * SB, (sb + 1) * SB)
            av_mm = 0
            ev_marks = [None] * 32

            def emit_qkt(tc):
                pp = tc % 3
                psS = (psSa, psSb, psSc)[pp]
                w = []
                if psS_free[pp] is not None:
                    w.extend(psS_free[pp])
                if tc < 3:
                    if hp == 0 and sb == 0:
                        w.append((act, p1_act))
                        w.append((dve, p1_dve))
                        w.append((pe, p1_pe))
                mma = P.t("matmul", psS[:, 0:512],
                          KV[0:64, hp, tc * 128:(tc + 1) * 128],
                          Q[0:64, hp, sslc], start=True, stop=True,
                          tile_position=(0, 0), waits=w)
                mmb = P.t("matmul", psS[:, 512:1024],
                          KV[64:128, hp, tc * 128:(tc + 1) * 128],
                          Q[64:128, hp, sslc], start=True, stop=True,
                          tile_position=(64, 0))
                ewa = [(pe, mma)]
                ewb = [(pe, mmb)]
                if sr_free[pp] > 0:
                    ewa.append((pe, sr_free[pp]))
                    ewb.append((pe, sr_free[pp]))
                mva = P.v("tensor_scalar", sr[pp][:, 0:512], psS[:, 0:512],
                          0.0, None, ALU.max, waits=ewa)
                mvb = P.s("activation", sr[pp][:, 512:1024],
                          psS[:, 512:1024], AF.Relu, waits=ewb)
                ev_marks[tc] = (mva, mvb)
                psS_free[pp] = [(dve, mva), (act, mvb)]

            def emit_av(tc):
                nonlocal av_mm
                pp = tc % 3
                mva, mvb = ev_marks[tc]
                w2 = [(dve, mva), (act, mvb)]
                if tc == 0:
                    w2.append(vl_mark)
                    if av_ev_a:
                        w2.append((dve, av_ev_a))
                    if av_ev_b:
                        w2.append((act, av_ev_b))
                P.t("matmul", psA[0:64, :], vtm[:, tc, 0:64],
                    sr[pp][:, 0:512], start=(tc == 0), stop=(tc == 31),
                    tile_position=(0, 0), waits=w2)
                av_mm = P.t("matmul", psB[64:128, :], vtm[:, tc, 64:128],
                            sr[pp][:, 512:1024], start=(tc == 0),
                            stop=(tc == 31), tile_position=(0, 64))
                sr_free[pp] = av_mm

            emit_qkt(0)
            emit_qkt(1)
            for tc in range(2, 32):
                emit_qkt(tc)
                emit_av(tc - 2)
            emit_av(30)
            emit_av(31)
            pq = sb & 1
            ew = [(pe, av_mm)]
            if ao_free[pq] > 0:
                ew.append((sao_sems[pq], ao_free[pq]))
            av_ev_a = P.v("tensor_copy", ao_sb[pq][0:64, :], psA[0:64, :],
                          waits=ew)
            av_ev_b = P.s("copy", ao_sb[pq][64:128, :],
                          psB[64:128, :], waits=ew)
            sao_cnt[pq] = P.dma("sync",
                                d_ao[hp * 128:(hp + 1) * 128, sslc],
                                ao_sb[pq][:, :], sao_sems[pq],
                                waits=[(dve, av_ev_a), (act, av_ev_b)])
            ao_free[pq] = sao_cnt[pq]
        av_prev_hp = av_mm

    attn_pe = P.sems["pe"].n
    attn_dve = P.sems["dve"].n
    attn_act = P.sems["act"].n

    woutb = T["woutb"]
    colsum_b = T["colsum_b"]
    nwb = P.dma("gpsimd", woutb, T["woutb_d"].rearrange(
        "(a p) d -> p a d", p=128), "s_wb", waits=[(pe, attn_pe)])
    mcs = 0
    cm = 0
    for half in range(2):
        dst = (psSa, psSb)[half]
        for c in range(CH):
            w = ()
            if c == 0:
                w = [("s_wb", nwb), (dve, attn_dve), (act, attn_act)]
            cm = P.t("matmul", dst[0:1, 0:512], onec,
                     woutb[:, c, half * 512:(half + 1) * 512],
                     start=(c == 0), stop=(c == CH - 1), waits=w)
        mcs = P.v("tensor_copy", colsum_b[:, half * 512:(half + 1) * 512],
                  dst[0:1, 0:512],
                  waits=[(pe, cm)] + ([(dve, mcs)] if mcs else []))

    w3 = {}
    for (name, dst, src) in T["castw_loads_late"]:
        w3[name] = P.dma("gpsimd", dst, src, "s_" + name,
                         waits=[(pe, attn_pe), (pe, cm)])

    if dbg == 2:
        dq = P.dma("sync", T["dbg_cs"], colsum_b[:, :], "s_y0",
                   waits=[(dve, mcs)])
        P.wait("sync", ("s_y0", dq),
               ("s_sao0", sao_cnt[0]), ("s_sao1", sao_cnt[1]))
        return

    aot = T["aot"]
    xf3 = T["xf3"]
    d_x1 = T["d_x1"]
    wout = T["wout"]
    v_bout = T["v_bout"]
    evp = [("dve", attn_dve), ("act", attn_act)]
    evpC = [psS_free[2][0], psS_free[2][1]]
    banksA = (T["psA"][:, 0:TB3], T["psB"][:, 0:TB3])
    banksC = (T["psSc"][:, 0:512], T["psSc"][:, 512:1024])
    aotb = [aot, T["aot2"]]
    ao_sems = ["s_ao3a", "s_ao3b"]
    ao_lds = [0, 0]
    aot_freem = [0, 0]
    x1_st = [0, 0]
    x1_ev = [0, 0]
    ld25 = [0, 0]
    n_r2 = 0

    def ao_load(sb):
        ab = sb & 1
        aw = [("s_sao0", sao_cnt[0]), ("s_sao1", sao_cnt[1]),
              (pe, attn_pe)]
        if aot_freem[ab] > 0:
            aw.append((pe, aot_freem[ab]))
        ao_lds[ab] = P.dma("sync", aotb[ab],
                           d_ao.rearrange("(a p) t -> p a t", p=128)
                           [:, :, sb * SB:(sb + 1) * SB],
                           ao_sems[ab], waits=aw)

    for sb in range(NSB):
        b = sb & 1
        sslc = slice(sb * SB, (sb + 1) * SB)
        if sb == 0:
            ao_load(0)
            ao_load(1)
        elif sb + 1 < NSB:
            ao_load(sb + 1)
        w = []
        if x1_st[b] > 0:
            w.append((("s_y0", "s_y1")[b], x1_st[b]))
        ld25[b] = P.dma("sync", xf3[b],
                        xT.rearrange("(a p) t -> p a t", p=128)[:, :, sslc],
                        ("s_x3a", "s_x3b")[b], waits=w)
        last = 0
        for dj in range(CH):
            pq = dj & 1
            psq = (psA, psB)[pq]
            w = [(ao_sems[b], ao_lds[b]), ("s_w2", n_w2),
                 (("s_x3a", "s_x3b")[b], ld25[b]), evp[pq]]
            mm = 0
            for c in range(CH):
                mm = P.t("matmul", psq[:, 0:SB],
                         wout[:, c, dj * 128:(dj + 1) * 128],
                         aotb[b][:, c, :], start=(c == 0),
                         stop=(c == CH - 1),
                         waits=(w if c == 0 else ()))
            if dj == CH - 1:
                aot_freem[b] = mm
            last = P.v("scalar_tensor_tensor", xf3[b][:, dj, :],
                       psq[:, 0:SB], v_bout[:, dj:dj + 1], xf3[b][:, dj, :],
                       ALU.add, ALU.add, waits=[(pe, mm)])
            evp[pq] = (dve, last)
        x1_ev[b] = last
        x1_st[b] = P.dma("sync", d_x1.rearrange("(a p) t -> p a t", p=128)
                         [:, :, sslc], xf3[b], ("s_y0", "s_y1")[b],
                         waits=[(dve, last)])
        rw = [(dve, last)]
        if sb > 0:
            rw.append(("s_x1s", n_r2))
        rmk = emit_ln_stats(xf3[b], SB, rw,
                            psSa[0:1, 512:1024], psSb[0:1, 512:1024],
                            None, None)
        n_r2 = P.dma("sync", T["d_r2"][0:1, sb * 1024:(sb + 1) * 1024],
                     rows_bf[:, 0:1024], "s_x1s", waits=[(dve, rmk)])
        rows_free = rmk
    x1_rend = [("s_y0", x1_st[0]), ("s_y1", x1_st[1])]
    r2_rend = ("s_x1s", n_r2)

    LNxp = T["LNx"]
    wrT, wkT, wvT = T["wrT"], T["wkT"], T["wvT"]
    wcat = T["wcat"]
    bcat = T["bcat"]
    v_boutb = T["v_boutb"]
    v_mur, v_muk = T["v_mur"], T["v_muk"]
    bcol2, bcol4 = T["bcol2"], T["bcol4"]
    idf = T["idf"]
    tab = T["tab"]
    gout = T["gout"]
    idxu = T["idxu"]
    extmask = T["extmask"]
    blkd8, blkdT8 = T["blkd8"], T["blkdT8"]
    x0row = T["x0row"]
    d_x0 = T["d_x0"]
    gtm_bf = T["gtm_bf"]
    gfm = T["gfm"]
    e_bf, R_bf, gw_bf, s_bf = T["e_bf"], T["R_bf"], T["gw_bf"], T["s_bf"]
    ta, tb_, tc2, td, tneg = T["ta"], T["tb"], T["tc2"], T["td"], T["tneg"]
    cint = T["cint"]
    yT = T["yT"]

    mz = 0
    for bc_t in (bcol2[0], bcol2[1], bcol4[0], bcol4[1]):
        mz = P.v("memset", bc_t, 0.0, waits=[(dve, attn_dve)])
    mz_done = mz

    ld3_sems = ["s_x3a", "s_x3b"]
    ld3_mark = list(ld25)
    xf3_free = list(x1_st)
    y_sems = ["s_y0", "s_y1"]
    y_cnt = list(x1_st)
    sa1 = ("dve", mcs)
    sb1 = ("dve", mcs)
    sc1 = ("act", attn_act)
    sc2 = ("act", attn_act)
    tab_ready = 0
    st8 = {}

    r2_sems = ["s_r2a", "s_r2b"]
    r2_mark = [0, 0]
    r2_free = [0, 0]

    def S_loads(t):
        b = t & 1
        w = list(x1_rend)
        if xf3_free[b] > 0:
            w.append((y_sems[b], xf3_free[b]))
        ld3_mark[b] = P.dma("sync", xf3[b],
                            d_x1.rearrange("(a p) t -> p a t", p=128)
                            [:, :, t * TB3:(t + 1) * TB3],
                            ld3_sems[b], waits=w)
        rw = [r2_rend]
        if r2_free[b] > 0:
            rw.append((pe, r2_free[b]))
        r2_mark[b] = P.dma("sync", T["rows2"][b],
                           T["d_r2"][0:1, t * 1024:(t + 1) * 1024],
                           r2_sems[b], waits=rw)

    def emit_ln(t, xwaits):
        x = xf3[t & 1]
        lnx = LNxp[t & 1]
        rmk = emit_ln_stats(x, TB3, xwaits,
                            psSa[0:1, 512:1024], psSb[0:1, 512:1024],
                            None, None, cast_pool=False)
        bc = emit_ln_bcast(TB3, rmk, psSa[:, 512:1024],
                           psSb[:, 512:1024])
        rsB = psSa[:, 512:1024].unsqueeze(1).broadcast_to([128, CH, TB3])
        nmB = psSb[:, 512:1024].unsqueeze(1).broadcast_to([128, CH, TB3])
        mk = P.v("tensor_tensor", lnx, x, rsB, ALU.mult,
                 waits=[(pe, bc), (dve, mz_done)] + xwaits)
        mk = P.v("tensor_tensor", lnx, lnx, nmB, ALU.add,
                 waits=[(dve, mk)])
        return mk

    def emit_ffn(t, ln_m, bc_rd, bc_wr, xin_m, banks, evp):
        b = t & 1
        x = xf3[b]
        lnx = LNxp[b]
        xr, xk = T["xr"][b], T["xk"][b]
        kr, r_bf16 = T["kr"][b], T["r_bf16"][b]
        mt = P.v("tensor_copy", bc_wr, lnx[:, :, TB3 - 1:TB3],
                 waits=[(dve, ln_m)])
        md = P.v("tensor_tensor", kr[:, :, 1:TB3], lnx[:, :, 1:TB3],
                 lnx[:, :, 0:TB3 - 1], ALU.subtract, waits=[(dve, mt)])
        md = P.v("tensor_tensor", kr[:, :, 0:1], lnx[:, :, 0:1],
                 bc_rd, ALU.subtract, waits=[(dve, md)])
        msc = 0
        mxr = md
        for c in range(CH):
            ew = [(dve, md)]
            if msc:
                ew.append((act, msc))
            msc = P.s("activation", xr[:, c, 1:TB3], kr[:, c, 1:TB3],
                      AF.Copy, scale=v_mur[:, c:c + 1], waits=ew)
            mxr = P.v("tensor_tensor", xr[:, c, 1:TB3], xr[:, c, 1:TB3],
                      lnx[:, c, 0:TB3 - 1], ALU.add,
                      waits=[(act, msc), (dve, mxr)])
            mxr = P.v("scalar_tensor_tensor", xr[:, c, 0:1],
                      kr[:, c, 0:1], v_mur[:, c:c + 1],
                      bc_rd[:, c, :], ALU.mult, ALU.add,
                      waits=[(dve, mxr)])
        mxk = mxr
        for c in range(CH):
            msc = P.s("activation", xk[:, c, 1:TB3], kr[:, c, 1:TB3],
                      AF.Copy, scale=v_muk[:, c:c + 1],
                      waits=[(act, msc), (dve, md)])
            mxk = P.v("tensor_tensor", xk[:, c, 1:TB3], xk[:, c, 1:TB3],
                      lnx[:, c, 0:TB3 - 1], ALU.add,
                      waits=[(act, msc), (dve, mxk)])
            mxk = P.v("scalar_tensor_tensor", xk[:, c, 0:1],
                      kr[:, c, 0:1], v_muk[:, c:c + 1],
                      bc_rd[:, c, :], ALU.mult, ALU.add,
                      waits=[(dve, mxk)])
        mex = 0
        for dj in range(CH):
            pq = dj & 1
            w = [(dve, mxr), ("s_wrT", w3["wrT"]), evp[pq]]
            mm = 0
            for c in range(CH):
                mm = P.t("matmul", banks[pq],
                         wrT[:, c, dj * 128:(dj + 1) * 128],
                         xr[:, c, :], start=(c == 0), stop=(c == CH - 1),
                         waits=(w if c == 0 else ()))
            ew = [(pe, mm)]
            if mex:
                ew.append((act, mex))
            mex = P.s("activation", r_bf16[:, dj, :],
                      banks[pq], AF.Exp, scale=-1.0, waits=ew)
            evp[pq] = (act, mex)
        mkr = 0
        for dj in range(CH):
            pq = dj & 1
            w = [(dve, mxk), ("s_wkT", w3["wkT"]), evp[pq]]
            mm = 0
            for c in range(CH):
                mm = P.t("matmul", banks[pq],
                         wkT[:, c, dj * 128:(dj + 1) * 128],
                         xk[:, c, :], start=(c == 0), stop=(c == CH - 1),
                         waits=(w if c == 0 else ()))
            mkr = P.s("activation", kr[:, dj, :],
                      banks[pq], AF.Relu,
                      waits=[(pe, mm), (dve, mxk) if dj == 0 else None])
            evp[pq] = (act, mkr)
        mln = P.s("activation", r_bf16, r_bf16, AF.Ln,
                  bias=T["cvals"][:, 1:2], waits=[(act, mkr)])
        mr = P.s("activation", r_bf16, r_bf16, AF.Exp, scale=-1.0,
                 waits=[(act, mln)])
        mka = P.v("tensor_tensor", kr, kr, kr, ALU.mult,
                  waits=[(act, mkr)])
        mfin = 0
        for dj in range(CH):
            pq = dj & 1
            w = [(dve, mka), ("s_wvT", w3["wvT"]), evp[pq]]
            mm = 0
            for c in range(CH):
                mm = P.t("matmul", banks[pq],
                         wvT[:, c, dj * 128:(dj + 1) * 128],
                         kr[:, c, :], start=(c == 0), stop=(c == CH - 1),
                         waits=(w if c == 0 else ()))
            mt1 = P.v("scalar_tensor_tensor", xr[:, dj, :],
                      r_bf16[:, dj, :], 1.0, banks[pq],
                      ALU.mult, ALU.mult,
                      waits=[(pe, mm), (act, mr), (dve, xin_m),
                             (dve, mka)])
            mfin = P.v("tensor_tensor", x[:, dj, :], x[:, dj, :],
                       xr[:, dj, :], ALU.add, waits=[(dve, mt1)])
            evp[pq] = (dve, mfin)
        return mfin

    def S5a(t):
        nonlocal sa1, sc1, tab_ready
        b = t & 1
        lnx = LNxp[b]
        ln3 = st8[(t, "ln3")]
        if t == 0:
            mrow = 0
            for a in range(CH):
                wv = [(dve, ln3), sc1] if a == 0 else [(dve, mrow)]
                mtp = P.t("transpose", psc_bf[0:1, 0:128],
                          lnx[:, a, 0:1], idb, waits=wv)
                mrow = P.v("tensor_copy", x0row[:, a * 128:(a + 1) * 128],
                           psc_bf[0:1, 0:128], waits=[(pe, mtp)])
            sc1 = ("dve", mrow)
            m1 = P.dma("sync", d_x0, x0row[:, :], "s_x0",
                       waits=[(dve, mrow)])
            tab_ready = P.dma("sync", tab[:, :],
                              d_x0[0:1, :].broadcast_to([128, 1024]),
                              "s_x0", waits=[("s_x0", m1)])
        w = [(dve, ln3), ("s_w2", n_w2), sa1]
        if t == 0:
            w.append(sc1)
        mm = 0
        for c in range(CH):
            mm = P.t("matmul", psSa[0:64, 0:512], wcat[:, c, :],
                     lnx[:, c, :], start=(c == 0), stop=(c == CH - 1),
                     waits=(w if c == 0 else ()))
        moff = P.v("tensor_scalar", ta[0:32, :], psSa[0:32, 0:512],
                   bcat[0:32, :], None, ALU.add, waits=[(pe, mm)])
        me = P.s("activation", tb_[0:32, :], psSa[32:64, 0:512], AF.Exp,
                 bias=bcat[32:64, :], waits=[(pe, mm)])
        st8[(t, "sa1r")] = [(dve, moff), (act, me)]
        mk = P.v("tensor_copy", cint[0:32, :], ta[0:32, :],
                 waits=[(dve, moff)])
        mk = P.v("tensor_copy", tc2[0:32, :], cint[0:32, :],
                 waits=[(dve, mk)])
        mk = P.v("tensor_tensor", td[0:32, :], tc2[0:32, :], ta[0:32, :],
                 ALU.is_gt, waits=[(dve, mk)])
        mk = P.v("tensor_tensor", tc2[0:32, :], tc2[0:32, :], td[0:32, :],
                 ALU.subtract, waits=[(dve, mk)])
        mk = P.v("tensor_tensor", td[0:32, :], tc2[0:32, :], ta[0:32, :],
                 ALU.not_equal, waits=[(dve, mk)])
        mk = P.v("tensor_scalar", tneg[0:32, :], ta[0:32, :], 0.0, None,
                 ALU.is_lt, waits=[(dve, mk)])
        mk = P.v("tensor_tensor", td[0:32, :], td[0:32, :], tneg[0:32, :],
                 ALU.mult, waits=[(dve, mk)])
        mk = P.v("tensor_tensor", ta[0:32, :], tc2[0:32, :], td[0:32, :],
                 ALU.add, waits=[(dve, mk)])
        mk = P.v("tensor_scalar", td[0:32, :], ta[0:32, :], 0.0, None,
                 ALU.is_lt, waits=[(dve, mk)])
        mk = P.v("scalar_tensor_tensor", ta[0:32, :], td[0:32, :], 1024.0,
                 ta[0:32, :], ALU.mult, ALU.add, waits=[(dve, mk)])
        midx = P.v("tensor_scalar", ta[0:32, :], ta[0:32, :], 1023.0, 0.0,
                   ALU.min, ALU.max, waits=[(dve, mk)])
        nonlocal sc2
        mtp_prev = 0
        tp_free = [sc1, sc2]
        for q in range(4):
            bsl = (psSa[:, 64:96], psSb[:, 64:96])[q & 1]
            wv = [tp_free[q & 1]]
            if q == 0:
                wv.append((dve, midx))
                wv.append(st8[(t, "sa1r")][1])
            mtp = P.t("transpose", bsl,
                      ta[0:32, q * 128:(q + 1) * 128], idf[:, :],
                      waits=wv)
            mtp_prev = P.v("tensor_copy", idxu[:, q, :], bsl,
                           waits=[(pe, mtp)] + ([(dve, mtp_prev)]
                                                if mtp_prev else []))
            tp_free[q & 1] = (dve, mtp_prev)
        sc1 = ("dve", mtp_prev)
        sc2 = tp_free[1]
        nonlocal sb1
        me = st8[(t, "sa1r")][1]
        meb = P.v("tensor_copy", e_bf[0:32, :], tb_[0:32, :], waits=[me])
        mS = P.t("matmul", psSb[0:8, 0:512], blkd8, e_bf[0:32, :],
                 start=True, stop=True, waits=[(dve, meb), sb1])
        mlnS = P.s("activation", td[0:8, :], psSb[0:8, 0:512], AF.Ln,
                   waits=[(pe, mS), (dve, midx)])
        mR = P.s("activation", tc2[0:8, :], td[0:8, :], AF.Exp, scale=-1.0,
                 waits=[(act, mlnS), (dve, mtp_prev)])
        mRb = P.v("tensor_copy", R_bf[0:8, :], tc2[0:8, :],
                  waits=[(act, mR)])
        mRB = P.t("matmul", psSb[0:32, 0:512], blkdT8, R_bf[0:8, :],
                  start=True, stop=True, waits=[(dve, mRb), (act, mlnS)])
        sb1 = ("dve", mRb)
        st8[(t, "meb")] = meb
        st8[(t, "mR")] = mR
        st8[(t, "mRB")] = mRB
        gq = T["gq"]
        mmul = 0
        for q in range(4):
            wv = [(dve, mtp_prev), ("s_x0", tab_ready)]
            if mmul:
                wv.append((pool, mmul))
            mgth = P.g("indirect_copy", gout[:, :], tab[:, :],
                       idxu[:, q, :], True, waits=wv)
            mmul = P.g("tensor_tensor", gq[:, q, :], gout[:, :],
                       extmask[:, :], ALU.mult,
                       waits=[(pool, mgth), ("s_em", w3["em"])])
        st8[(t, "gmul")] = mmul

    def S5b(t):
        nonlocal sa1, sb1, sc1, sc2, evp
        b = t & 1
        x = xf3[b]
        mred = 0
        for q in range(4):
            mred = P.v("tensor_reduce", T["gtm32"][:, q * 32:(q + 1) * 32],
                       T["gq"][:, q, :].rearrange("p (h q) -> p h q", q=16),
                       mybir.AxisListType.X, ALU.add,
                       waits=[(pool, st8[(t, "gmul")])] +
                             ([(dve, mred)] if mred else []))
        gcast = P.v("tensor_copy", gtm_bf[:, 0:128], T["gtm32"][:, 0:128],
                    waits=[(dve, mred)])
        ffn1_m = st8[(t, "ffn1")]
        mgf = 0
        for q in range(4):
            wv = [(dve, gcast), sc1]
            if mgf:
                wv.append((dve, mgf))
            mtp = P.t("transpose", psc_bf[0:32, 0:128],
                      gtm_bf[:, q * 32:(q + 1) * 32], idb, waits=wv)
            mgf = P.v("tensor_copy", gfm[:, q * 128:(q + 1) * 128],
                      psc_bf[0:32, 0:128], waits=[(pe, mtp)])
        sc1 = ("dve", mgf)
        meb = st8[(t, "meb")]
        mRB = st8[(t, "mRB")]
        mR = st8[(t, "mR")]
        mk = P.v("tensor_tensor", td[0:32, :], gfm[:, :], e_bf[0:32, :],
                 ALU.mult, waits=[(dve, mgf), (dve, meb), (act, mR)])
        mk = P.v("tensor_tensor", gw_bf[0:32, :], td[0:32, :],
                 psSb[0:32, 0:512], ALU.mult, waits=[(dve, mk), (pe, mRB)])
        ms = P.t("matmul", psSb[0:1, 0:512], onec[0:32, :],
                 gw_bf[0:32, :], start=True, stop=True, waits=[(dve, mk)])
        msb = P.v("tensor_copy", s_bf[:, :], psSb[0:1, 0:512],
                  waits=[(pe, ms)])
        sb1 = ("dve", msb)
        mab = 0
        for dj in range(CH):
            po = dj & 1
            dst = (psSa[:, 0:512], psSb[:, 0:512])[po]
            w = [(dve, msb), (dve, mcs), (sa1 if po == 0 else sc2), CST]
            if dj < 2:
                w.extend(st8[(t, "sa1r")])
            mm = P.t("matmul", dst, colsum_b[:, dj * 128:(dj + 1) * 128],
                     s_bf[:, :], start=True, stop=True, waits=w)
            mab = P.v("scalar_tensor_tensor", x[:, dj, :], dst,
                      v_boutb[:, dj:dj + 1], x[:, dj, :], ALU.add, ALU.add,
                      waits=[(pe, mm), (dve, ffn1_m)])
            if po == 0:
                sa1 = ("dve", mab)
            else:
                sc2 = ("dve", mab)
        st8[(t, "mab")] = mab

    def S8(t):
        b = t & 1
        y_cnt[b] = P.dma("sync",
                         yT.rearrange("(a p) t -> p a t", p=128)
                         [:, :, t * TB3:(t + 1) * TB3],
                         xf3[b], y_sems[b],
                         waits=[(dve, st8[(t, "ffn2")])])
        xf3_free[b] = y_cnt[b]

    def emit_ln2(t, prev_mk):
        b = t & 1
        x = xf3[b]
        lnx = LNxp[b]
        rw = [(r2_sems[b], r2_mark[b]), (dve, mz_done)]
        if prev_mk is not None:
            rw.append((dve, prev_mk))
        P.t("matmul", psSa[:, 512:1024], oner, T["rows2"][b][:, 512:1024],
            start=True, stop=True, waits=rw + [CST])
        bc = P.t("matmul", psSb[:, 512:1024], oner,
                 T["rows2"][b][:, 0:512], start=True, stop=True)
        r2_free[b] = bc
        rsB = psSa[:, 512:1024].unsqueeze(1).broadcast_to([128, CH, TB3])
        nmB = psSb[:, 512:1024].unsqueeze(1).broadcast_to([128, CH, TB3])
        mk = P.v("tensor_tensor", lnx, x, rsB, ALU.mult,
                 waits=[(pe, bc), (ld3_sems[b], ld3_mark[b]),
                        (dve, mz_done)])
        mk = P.v("tensor_tensor", lnx, lnx, nmB, ALU.add,
                 waits=[(dve, mk)])
        return mk

    S_loads(0)
    S_loads(1)
    st8[(0, "ln2")] = emit_ln2(0, rows_free)
    st8[(0, "ffn1")] = emit_ffn(0, st8[(0, "ln2")], bcol2[0], bcol2[1],
                                st8[(0, "ln2")], banksA, evp)
    st8[(0, "ln3")] = emit_ln(0, [(dve, st8[(0, "ffn1")])])
    S5a(0)
    for t in range(1, NT3):
        st8[(t, "ln2")] = emit_ln2(t, st8[(t - 1, "ln3")])
        st8[(t, "ffn1")] = emit_ffn(t, st8[(t, "ln2")], bcol2[t & 1],
                                    bcol2[(t + 1) & 1], st8[(t, "ln2")],
                                    banksA, evp)
        S5b(t - 1)
        st8[(t - 1, "ln4")] = emit_ln(t - 1, [(dve, st8[(t - 1, "mab")])])
        st8[(t - 1, "ffn2")] = emit_ffn(t - 1, st8[(t - 1, "ln4")],
                                        bcol4[(t - 1) & 1], bcol4[t & 1],
                                        st8[(t - 1, "mab")], banksC, evpC)
        S8(t - 1)
        if t + 1 < NT3:
            S_loads(t + 1)
        st8[(t, "ln3")] = emit_ln(t, [(dve, st8[(t, "ffn1")])])
        S5a(t)
    tl = NT3 - 1
    S5b(tl)
    st8[(tl, "ln4")] = emit_ln(tl, [(dve, st8[(tl, "mab")])])
    st8[(tl, "ffn2")] = emit_ffn(tl, st8[(tl, "ln4")],
                                 bcol4[tl & 1], bcol4[(tl + 1) & 1],
                                 st8[(tl, "mab")], banksC, evpC)
    S8(tl)

    P.wait("sync", ("s_y0", y_cnt[0]), ("s_y1", y_cnt[1]))


def build_nc(dbg=0):
    nc = bass.Bass("TRN2", target_bir_lowering=False)
    T = {}

    def din(name, shape, dt=F32):
        return nc.dram_tensor(name, shape, dt, kind="ExternalInput").ap()

    T["xT"] = din("xT", [D, S])
    for nm in ("wq_d", "wkv_d", "wout_d", "wrT_d", "wkT_d", "wvT_d",
               "woutb_d"):
        T[nm] = din(nm, [D, D])
    T["wcat_d"] = din("wcat_d", [D, 64])
    T["bcat_d"] = din("bcat_d", [64, 1])
    T["bout_d"] = din("bout_d", [128, CH])
    T["boutb_d"] = din("boutb_d", [128, CH])
    T["mur_d"] = din("mur_d", [128, CH])
    T["muk_d"] = din("muk_d", [128, CH])
    T["idb_d"] = din("idb_d", [128, 128], BF16)
    T["idf_d"] = din("idf_d", [32, 32], F32)
    T["onec_d"] = din("onec_d", [128, 1], BF16)
    T["onem_d"] = din("onem_d", [128, 1], BF16)
    T["oner_d"] = din("oner_d", [1, 128], BF16)
    T["blkd8_d"] = din("blkd8_d", [32, 8], BF16)
    T["blkdT8_d"] = din("blkdT8_d", [8, 32], BF16)
    T["extmask_d"] = din("extmask_d", [128, 512], BF16)
    T["cvals_d"] = din("cvals_d", [128, 2], F32)

    T["yT"] = nc.dram_tensor("yT", [D, S], F32, kind="ExternalOutput").ap()
    T["d_ao"] = nc.dram_tensor("d_ao", [D, S], BF16,
                               kind=("ExternalOutput" if dbg == 2 else "Internal")).ap()
    T["d_x0"] = nc.dram_tensor("d_x0", [1, D], BF16, kind="Internal").ap()
    T["d_vtm"] = nc.dram_tensor("d_vtm", [S, D], BF16, kind="Internal").ap()
    T["d_x1"] = nc.dram_tensor("d_x1", [D, S], F32, kind="Internal").ap()
    T["d_r2"] = nc.dram_tensor("d_r2", [1, 2 * S], BF16, kind="Internal").ap()
    if dbg == 1:
        T["dbg_q"] = nc.dram_tensor("dbg_q", [128, CH, S], BF16,
                                    kind="ExternalOutput").ap()
        T["dbg_kv"] = nc.dram_tensor("dbg_kv", [128, CH, S], BF16,
                                     kind="ExternalOutput").ap()
    if dbg == 2:
        T["dbg_cs"] = nc.dram_tensor("dbg_cs", [1, D], BF16,
                                     kind="ExternalOutput").ap()

    es = ExitStack()
    with es:
        arena = es.enter_context(nc.sbuf_tensor("arena", [128, AW], U8))
        ar = arena[:]

        def carve(off_kb, shape, dt):
            isz = {F32: 4, BF16: 2, U16: 2, I32: 4}[dt]
            n = 1
            for d in shape[1:]:
                n *= d
            off = int(off_kb * KB)
            assert off + n * isz <= AW, (off_kb, shape)
            v = ar[:, off:off + n * isz].bitcast(dt)
            if len(shape) == 3:
                v = v.rearrange("p (a n) -> p a n", a=shape[1])
            return v

        T["Q"] = carve(0, [128, CH, S], BF16)
        T["xf3"] = [carve(0, [128, CH, TB3], F32),
                    carve(16, [128, CH, TB3], F32)]
        T["LNx"] = [carve(32, [128, CH, TB3], BF16),
                    carve(40, [128, CH, TB3], BF16)]
        T["xr"] = [carve(48, [128, CH, TB3], BF16),
                   carve(56, [128, CH, TB3], BF16)]
        T["KV"] = carve(64, [128, CH, S], BF16)
        T["wrT"] = carve(64, [128, CH, D], BF16)
        T["wkT"] = carve(80, [128, CH, D], BF16)
        T["wvT"] = carve(96, [128, CH, D], BF16)
        T["xk"] = [carve(112, [128, CH, TB3], BF16),
                    carve(120, [128, CH, TB3], BF16)]
        T["aot"] = carve(112, [128, CH, TB3], BF16)
        T["aot2"] = carve(120, [128, CH, TB3], BF16)
        T["wq"] = carve(128, [128, CH, D], BF16)
        T["wkv"] = carve(144, [128, CH, D], BF16)
        T["xf1"] = [carve(160, [128, CH, TB1], F32),
                    carve(168, [128, CH, TB1], F32)]
        T["LNx1"] = carve(176, [128, CH, TB1], BF16)
        T["V_tm"] = [carve(128, [128, 32, 128], BF16),
                     carve(136, [128, 32, 128], BF16)]
        T["sr"] = [carve(144, [128, 1024], BF16),
                   carve(146, [128, 1024], BF16),
                   carve(148, [128, 1024], BF16)]
        T["ao_sb"] = [carve(150, [128, 512], BF16),
                      carve(151, [128, 512], BF16)]
        T["wout"] = carve(152, [128, CH, D], BF16)
        T["wcat"] = carve(172, [128, CH, 64], BF16)
        T["woutb"] = carve(64, [128, CH, D], BF16)
        T["kr"] = [carve(128, [128, CH, TB3], BF16),
                   carve(136, [128, CH, TB3], BF16)]
        T["r_bf16"] = [carve(144, [128, CH, TB3], BF16),
                       carve(152, [128, CH, TB3], BF16)]
        T["ta"] = carve(160, [128, TB3], F32)
        T["tb"] = carve(162, [128, TB3], F32)
        T["tc2"] = carve(164, [128, TB3], F32)
        T["td"] = carve(166, [128, TB3], F32)
        T["cint"] = carve(166, [128, TB3], I32)
        T["tneg"] = carve(168, [128, TB3], BF16)
        T["s_bf"] = carve(169, [128, 512], BF16)[0:1, :]
        T["colsum_b"] = carve(178, [128, 1024], BF16)[0:1, :]
        T["gfm"] = carve(173, [128, 512], BF16)[0:32, :]
        T["gw_bf"] = T["gfm"]
        T["x0row"] = carve(170, [128, 1024], BF16)[0:1, :]
        T["rows2"] = [carve(174, [128, 1024], BF16)[0:1, :],
                      carve(176, [128, 1024], BF16)[0:1, :]]

        def sb(name, shape, dt):
            return es.enter_context(nc.sbuf_tensor(name, shape, dt))[:]

        T["idb"] = sb("idb", [128, 128], BF16)
        T["idf"] = sb("idf", [32, 32], F32)
        T["ones_col"] = sb("ones_col", [128, 1], BF16)
        T["ones_mean"] = sb("ones_mean", [128, 1], BF16)
        T["ones_row"] = sb("ones_row", [1, 128], BF16)
        T["blkd8"] = sb("blkd8", [32, 8], BF16)
        T["blkdT8"] = sb("blkdT8", [8, 32], BF16)
        T["cvals"] = sb("cvals", [128, 2], F32)
        T["bcat"] = sb("bcat", [64, 1], F32)
        T["v_bout"] = sb("v_bout", [128, CH], F32)
        T["v_boutb"] = sb("v_boutb", [128, CH], F32)
        T["v_mur"] = sb("v_mur", [128, CH], F32)
        T["v_muk"] = sb("v_muk", [128, CH], F32)
        T["rows_bf"] = sb("rows_bf", [1, 1024], BF16)
        T["rows_f"] = sb("rows_f", [1, 1536], F32)
        T["xbt"] = [sb("xbt0", [128, 512], BF16),
                    sb("xbt1", [128, 512], BF16)]
        T["xsqt"] = [sb("xsqt0", [128, 512], BF16),
                     sb("xsqt1", [128, 512], BF16)]
        T["tab"] = sb("tab", [128, 1024], BF16)
        T["gout"] = sb("gout", [128, 512], BF16)
        T["extmask"] = sb("extmask", [128, 512], BF16)
        T["gtm32"] = sb("gtm32", [128, 128], F32)
        T["gq"] = sb("gq", [128, 4, 512], BF16)
        T["gout2"] = sb("gout2", [128, 512], BF16)
        T["vtm_sb"] = [sb("vtm0", [128, 512], BF16), sb("vtm1", [128, 512], BF16)]
        T["idxu"] = sb("idxu", [128, 4, 32], U16)

        T["gtm_bf"] = sb("gtm_bf", [128, 128], BF16)
        T["e_bf"] = sb("e_bf", [32, 512], BF16)
        T["R_bf"] = sb("R_bf", [8, 512], BF16)
        T["bcol2"] = [sb("bcol2a", [128, CH, 1], BF16),
                      sb("bcol2b", [128, CH, 1], BF16)]
        T["bcol4"] = [sb("bcol4a", [128, CH, 1], BF16),
                      sb("bcol4b", [128, CH, 1], BF16)]

        def ps(name, shape, dt=F32):
            return es.enter_context(nc.psum_tensor(name, shape, dt))[:]

        T["psA"] = ps("psA", [128, 512])
        T["psB"] = ps("psB", [128, 512])
        T["psSa"] = ps("psSa", [128, 1024])
        T["psSb"] = ps("psSb", [128, 1024])
        T["psSc"] = ps("psSc", [128, 1024])
        T["psc_bf"] = T["psSa"][:, 448:512].bitcast(BF16)
        T["psc_v"] = T["psSc"].bitcast(BF16)

        T["const_loads"] = [
            (T["idb"], T["idb_d"]), (T["idf"], T["idf_d"]),
            (T["ones_col"], T["onec_d"]), (T["ones_mean"], T["onem_d"]),
            (T["ones_row"], T["oner_d"]),
            (T["blkd8"], T["blkd8_d"]), (T["blkdT8"], T["blkdT8_d"]),
            (T["bcat"], T["bcat_d"]),
            (T["cvals"], T["cvals_d"]),
            (T["v_bout"], T["bout_d"]), (T["v_boutb"], T["boutb_d"]),
            (T["v_mur"], T["mur_d"]), (T["v_muk"], T["muk_d"]),
        ]

        def wview(dr):
            return dr.rearrange("(a p) d -> p a d", p=128)

        T["castw_loads_p0"] = [
            (T["wq"], wview(T["wq_d"])), (T["wkv"], wview(T["wkv_d"])),
        ]
        T["castw_loads_mid"] = [
            (T["wout"], wview(T["wout_d"])),
            (T["wcat"], wview(T["wcat_d"])),
        ]
        T["castw_loads_late"] = [
            ("wrT", T["wrT"], wview(T["wrT_d"])),
            ("wkT", T["wkT"], wview(T["wkT_d"])),
            ("wvT", T["wvT"], wview(T["wvT_d"])),
            ("em", T["extmask"], T["extmask_d"]),
        ]

        semnames = ["s_cst", "s_cstw", "s_w2", "s_wb", "s_wrT", "s_wkT",
                    "s_wvT", "s_em", "s_x1", "s_x2", "s_x3a", "s_x3b",
                    "s_ao3a", "s_ao3b",
                    "s_sao0", "s_sao1", "s_x0", "s_y0", "s_y1",
                    "s_vt0", "s_vt1", "s_vl0", "s_vl1", "s_x1s", "s_r2a", "s_r2b",
                    "pe", "act", "dve", "pool"]
        semh = {nm: es.enter_context(nc.semaphore(nm)) for nm in semnames}

        block = es.enter_context(nc.Block())
        for engname in ("gpsimd", "scalar", "vector", "tensor", "sync"):
            def runner(eng, engname=engname):
                sems = {nm: Sem(nm) for nm in semnames}
                for nm in semnames:
                    sems[nm].h = semh[nm]
                engines = dict.fromkeys(
                    ("gpsimd", "scalar", "vector", "tensor", "sync"))
                engines[engname] = eng
                schedule(Prog(engines, sems, engname), T, dbg)
            getattr(block, engname)(runner)
    return nc


def host_inputs(inp, b):
    x = np.asarray(inp["L0_attn_a_hidden_states"])
    m = {}
    m["xT"] = np.ascontiguousarray(x[b].T.astype(np.float32))
    m["wq_d"] = np.ascontiguousarray(np.asarray(inp["L0_attn_a_Wq"],
                                                np.float32))
    m["wkv_d"] = np.ascontiguousarray(np.asarray(inp["L0_attn_a_Wkv"],
                                                 np.float32))
    m["wout_d"] = np.ascontiguousarray(np.asarray(inp["L0_attn_a_Wout"],
                                                  np.float32))
    m["wrT_d"] = np.ascontiguousarray(
        np.asarray(inp["L0_ffn_W_r"], np.float32).T)
    m["wkT_d"] = np.ascontiguousarray(
        np.asarray(inp["L0_ffn_W_k"], np.float32).T)
    m["wvT_d"] = np.ascontiguousarray(
        np.asarray(inp["L0_ffn_W_v"], np.float32).T)
    m["woutb_d"] = np.ascontiguousarray(
        np.asarray(inp["L0_attn_b_W_out"], np.float32))
    woff = np.asarray(inp["L0_attn_b_W_off"], np.float32)
    watt = np.asarray(inp["L0_attn_b_W_attn"], np.float32)
    m["wcat_d"] = np.ascontiguousarray(
        np.concatenate([woff[:, 0::2], watt], axis=1))
    boff = np.asarray(inp["L0_attn_b_b_off"], np.float32)
    batt = np.asarray(inp["L0_attn_b_b_attn"], np.float32)
    m["bcat_d"] = np.ascontiguousarray(
        np.concatenate([boff[0::2], batt]).reshape(64, 1))

    def fm(v):
        return np.ascontiguousarray(
            np.asarray(v, np.float32).reshape(CH, 128).T)

    m["bout_d"] = fm(inp["L0_attn_a_bout"])
    m["boutb_d"] = fm(inp["L0_attn_b_b_out"])
    m["mur_d"] = fm(inp["L0_ffn_mu_r"])
    m["muk_d"] = fm(inp["L0_ffn_mu_k"])
    m["idb_d"] = np.eye(128).astype(_bf)
    m["idf_d"] = np.eye(32).astype(np.float32)
    m["onec_d"] = np.ones((128, 1), _bf)
    m["onem_d"] = np.full((128, 1), 1.0 / 1024.0, _bf)
    m["oner_d"] = np.ones((1, 128), _bf)
    blk = np.zeros((32, 8), np.float32)
    for h in range(8):
        blk[4 * h:4 * h + 4, h] = 1.0
    m["blkd8_d"] = blk.astype(_bf)
    m["blkdT8_d"] = np.ascontiguousarray(blk.T).astype(_bf)
    em = np.zeros((128, 512), np.float32)
    for p in range(128):
        for h in range(32):
            em[p, 16 * h + (p % 16)] = 1.0
    m["extmask_d"] = em.astype(_bf)
    cv = np.zeros((128, 2), np.float32)
    cv[:, 0] = EPS
    cv[:, 1] = 1.0
    m["cvals_d"] = cv
    return m


def kernel(**inputs):
    B = 8
    nc = build_nc(dbg=0)
    in_maps = [host_inputs(inputs, b) for b in range(B)]
    res = run_bass_kernel_spmd(nc, in_maps, core_ids=list(range(B)))
    out = np.empty((B, S, D), np.float32)
    for b in range(B):
        out[b] = np.asarray(res.results[b]["yT"]).T
    return out
